# version 01
# baseline (speedup 1.0000x reference)
import contextlib
import numpy as np
import ml_dtypes
import concourse.bass as bass
import concourse.mybir as mybir
from concourse.bass_utils import run_bass_kernel_spmd

F32 = mybir.dt.float32
BF16 = mybir.dt.bfloat16
I32 = mybir.dt.int32
AF = mybir.ActivationFunctionType
ALU = mybir.AluOpType
AX = mybir.AxisListType
BF = ml_dtypes.bfloat16

NCORES = 8
D = 2048
TOWN = 1024
TPAD = 1040
HALO = 3
TT3 = [(0, 512), (512, 512), (1024, 16)]


_CHAIN_OPS = frozenset({
    "activation", "square", "sqrt", "mul", "copy", "add", "memzero", "memset",
    "tensor_copy", "tensor_tensor", "tensor_add", "tensor_sub", "tensor_mul",
    "tensor_max", "tensor_scalar", "tensor_single_scalar", "tensor_scalar_add",
    "tensor_scalar_sub", "tensor_scalar_mul", "tensor_scalar_max",
    "tensor_scalar_min", "tensor_relu", "tensor_reduce", "reduce_max",
    "reduce_sum", "scalar_tensor_tensor", "tensor_tensor_scan", "select",
    "transpose", "bn_stats", "bn_aggr", "iota", "affine_select",
    "partition_broadcast", "reciprocal", "max", "pool", "pool_avg",
    "copy_predicated", "sign",
})


class Chained:

    def __init__(self, ctx, eng, name):
        self._ctx = ctx
        self._eng = eng
        self._name = name

    def __getattr__(self, attr):
        f = getattr(self._eng, attr)
        if attr in _CHAIN_OPS and callable(f):
            ctx, eng, name = self._ctx, self._eng, self._name

            def wrap(*a, **k):
                if ctx.cnt[name] > 0:
                    eng.wait_ge(ctx.sem(name), ctx.cnt[name])
                inst = f(*a, **k)
                inst.then_inc(ctx.sem(name), 1)
                ctx.cnt[name] += 1
                return inst
            return wrap
        return f

    def flush(self):
        ctx, eng, name = self._ctx, self._eng, self._name
        if ctx.cnt[name] > 0:
            eng.wait_ge(ctx.sem(name), ctx.cnt[name])

    def mark(self, mname, n=1):
        ctx, eng, name = self._ctx, self._eng, self._name
        if ctx.cnt[name] > 0:
            eng.wait_ge(ctx.sem(name), ctx.cnt[name])
        eng.sem_inc(ctx.sem(mname), n)
        ctx.cnt[mname] += n


class Ctx:
    def __init__(self, nc, es):
        self.nc = nc
        self.es = es
        self.sems = {}
        self.cnt = {}

    def sem(self, name):
        if name not in self.sems:
            self.sems[name] = self.es.enter_context(self.nc.semaphore(name))
            self.cnt[name] = 0
        return self.sems[name]

    def inc(self, inst, name, n=1):
        inst.then_inc(self.sem(name), n)
        self.cnt[name] += n
        return inst

    def dinc(self, inst, name):
        return self.inc(inst, name, 16)

    def wait(self, eng, name, cnt=None):
        eng.wait_ge(self.sem(name), self.cnt[name] if cnt is None else cnt)

    def ch(self, eng, name):
        self.sem(name)
        return Chained(self, eng, name)


def make_identity(gp, tile):
    gp.memset(tile, 0.0)
    gp.affine_select(out=tile, in_=tile, compare_op=ALU.not_equal, fill=1.0,
                     base=0, pattern=[[-1, tile.shape[1]]], channel_multiplier=1)



def build_launchA():
    nc = bass.Bass("TRN2", target_bir_lowering=False)
    xs = nc.dram_tensor("xs", [TPAD, D], F32, kind="ExternalInput").ap()
    wqkvz = nc.dram_tensor("wqkvz", [D, 12288], F32, kind="ExternalInput").ap()
    wba = nc.dram_tensor("wba", [128, 16 * 64], F32, kind="ExternalInput").ap()
    ln1 = nc.dram_tensor("ln1", [1, D], F32, kind="ExternalInput").ap()
    convw = nc.dram_tensor("convw", [128, 64 * 4], F32, kind="ExternalInput").ap()
    dtb = nc.dram_tensor("dtb", [32, 1], F32, kind="ExternalInput").ap()
    alog = nc.dram_tensor("alog", [32, 1], F32, kind="ExternalInput").ap()

    convq = nc.dram_tensor("convq", [2048, TOWN], BF16, kind="ExternalOutput").ap()
    convk = nc.dram_tensor("convk", [2048, TOWN], BF16, kind="ExternalOutput").ap()
    convv = nc.dram_tensor("convv", [4096, TOWN], BF16, kind="ExternalOutput").ap()
    zfm = nc.dram_tensor("zfm", [4096, TOWN], BF16, kind="ExternalOutput").ap()
    grow = nc.dram_tensor("grow", [32, TOWN], F32, kind="ExternalOutput").ap()
    brow = nc.dram_tensor("brow", [32, TOWN], F32, kind="ExternalOutput").ap()

    xsbf = nc.dram_tensor("xsbf", [TPAD, D], BF16).ap()

    def conv_out_of(fc):
        if fc < 16:
            return convq, fc
        if fc < 32:
            return convk, fc - 16
        return convv, fc - 32

    es = contextlib.ExitStack()
    with es:
        ctx = Ctx(nc, es)
        sb = lambda n, s, d: es.enter_context(nc.sbuf_tensor(n, s, d))

        xT = sb("xT", [128, 16 * TPAD], BF16)
        scrb = sb("scrb", [128, TPAD], BF16)
        onesb = sb("onesb", [128, 1], BF16)
        onesf = sb("onesf", [1, 128], F32)
        ln1row = sb("ln1row", [1, D], F32)
        ln1col = sb("ln1col", [128, 16], F32)
        identf = sb("identf", [128, 128], F32)
        invrow = sb("invrow", [1, TPAD], F32)
        invbc = sb("invbc", [128, TPAD], F32)
        wbuf = [sb(f"wbuf{i}", [128, 16 * 512], BF16) for i in range(2)]
        wbab = sb("wbab", [128, 16 * 64], BF16)
        cw = sb("cw", [128, 64 * 4], F32)
        qkT = [sb(f"qkT{i}", [128, TPAD], BF16) for i in range(3)]
        cacc = [sb(f"cacc{i}", [128, TOWN], F32) for i in range(2)]
        cout = [sb(f"cout{i}", [128, TOWN], BF16) for i in range(2)]
        zout = [sb(f"zout{i}", [128, TOWN], BF16) for i in range(2)]
        barows = sb("barows", [64, TOWN], F32)
        gw1 = sb("gw1", [32, TOWN], F32)
        gw2 = sb("gw2", [32, TOWN], F32)
        dtbt = sb("dtbt", [32, 1], F32)
        alogt = sb("alogt", [32, 1], F32)
        negea = sb("negea", [32, 1], F32)
        gout = sb("gout", [32, TOWN], F32)
        bout = sb("bout", [32, TOWN], F32)

        block = es.enter_context(nc.Block())

        ps_es = contextlib.ExitStack()
        psB = ps_es.enter_context(nc.psum_tensor("psB", [128, 512], F32))
        psR = [ps_es.enter_context(nc.psum_tensor(f"psR{i}", [1, n], F32))
               for i, (o, n) in enumerate(TT3)]

        def sync0(sync):
            ctx.dinc(sync.dma_start(out=ln1row[:, :], in_=ln1[:, :]), "dS")
            ctx.dinc(sync.dma_start(out=dtbt[:, :], in_=dtb[:, :]), "dS")
            ctx.dinc(sync.dma_start(out=alogt[:, :], in_=alog[:, :]), "dS")
            ctx.dinc(sync.dma_start(out=cw[:, :], in_=convw[:, :]), "dS")
        block.sync(sync0)

        def gp0(gp):
            g = ctx.ch(gp, "cG")
            g.memset(onesb[:, :], 1.0)
            g.memset(onesf[:, :], 1.0)
            make_identity(g, identf[:, :])
            g.mark("gE")
            ctx.dinc(gp.dma_start(out=xsbf[:, :], in_=xs[:, :]), "dXc")
            ctx.dinc(gp.dma_start(out=wbab[:, :], in_=wba[:, :]), "dWba")
        block.gpsimd(gp0)

        def sync1(sync):
            ctx.wait(sync, "dXc")
            for j in range(16):
                ctx.dinc(sync.dma_start(out=xT[:, TPAD * j:TPAD * (j + 1)],
                                        in_=xsbf[:, 128 * j:128 * (j + 1)],
                                        transpose=True), "dT")
        block.sync(sync1)

        def t0(t):
            ctx.wait(t, "dS")
            ctx.wait(t, "gE")
            for j in range(16):
                if j >= 1:
                    ctx.wait(t, "aLn", j)
                ctx.inc(t.transpose(psB[:, 0:1], ln1row[:, 128 * j:128 * (j + 1)],
                                    identf[0:1, 0:1]), "tLn")
        block.tensor(t0)

        def a0(a):
            a = ctx.ch(a, "cA")
            for j in range(16):
                ctx.wait(a, "tLn", j + 1)
                a.activation(ln1col[:, j:j + 1], psB[:, 0:1], AF.Copy)
                a.mark("aLn")
        block.scalar(a0)

        def a1(a):
            a = ctx.ch(a, "cA")
            ctx.wait(a, "dT")
            for j in range(16):
                if j >= 1:
                    ctx.wait(a, "tSq", 3 * j)
                a.activation(scrb[:, :], xT[:, TPAD * j:TPAD * (j + 1)], AF.Square)
                a.mark("aSq")
        block.scalar(a1)

        def t1(t):
            for j in range(16):
                ctx.wait(t, "aSq", j + 1)
                for ti, (t0_, tn) in enumerate(TT3):
                    ctx.inc(t.matmul(psR[ti][:, 0:tn], onesb[:, :],
                                     scrb[:, t0_:t0_ + tn],
                                     start=(j == 0), stop=(j == 15)), "tSq")
        block.tensor(t1)

        def a2(a):
            a = ctx.ch(a, "cA")
            ctx.wait(a, "tSq")
            for ti, (t0_, tn) in enumerate(TT3):
                a.activation(invrow[:, t0_:t0_ + tn], psR[ti][0:1, 0:tn], AF.Copy)
            a.mark("aEv")
        block.scalar(a2)

        def v_inv(v):
            v = ctx.ch(v, "cV")
            ctx.wait(v, "aEv")
            v.tensor_scalar(invrow[:, :], invrow[:, :], 1.0 / D, 1e-6,
                            ALU.mult, ALU.add)
            v.mark("vInv")
        block.vector(v_inv)

        def a2b(a):
            a = ctx.ch(a, "cA")
            ctx.wait(a, "aEv")
            ctx.wait(a, "vInv")
            a.activation(invrow[:, :], invrow[:, :], AF.Ln)
            a.activation(invrow[:, :], invrow[:, :], AF.Exp, scale=-0.5)
            a.mark("aInv")
        block.scalar(a2b)

        def t2(t):
            ctx.wait(t, "aInv")
            for ti, (t0_, tn) in enumerate(TT3):
                if ti >= 1:
                    ctx.wait(t, "aBc", ti)
                ctx.inc(t.matmul(psB[:, 0:tn], onesf[:, :], invrow[:, t0_:t0_ + tn],
                                 start=True, stop=True), "tBc")
        block.tensor(t2)

        def a3(a):
            a = ctx.ch(a, "cA")
            for ti, (t0_, tn) in enumerate(TT3):
                ctx.wait(a, "tBc", ti + 1)
                a.activation(invbc[:, t0_:t0_ + tn], psB[:, 0:tn], AF.Copy)
                a.mark("aBc")
        block.scalar(a3)

        def v1(v):
            v = ctx.ch(v, "cV")
            ctx.wait(v, "aBc")
            ctx.wait(v, "aLn")
            for j in range(16):
                v.scalar_tensor_tensor(
                    xT[:, TPAD * j:TPAD * (j + 1)], xT[:, TPAD * j:TPAD * (j + 1)],
                    ln1col[:, j:j + 1], invbc[:, :], ALU.mult, ALU.mult)
            v.mark("vH")
        block.vector(v1)

        ps_es.close()

        psA = [es.enter_context(nc.psum_tensor(f"psA{i}", [128, 512], F32))
               for i in range(4)]
        psT = [es.enter_context(nc.psum_tensor(f"psT{i}", [128, 16], F32))
               for i in range(2)]

        NFC = 96

        def gpW(gp):
            for g in range(24):
                if g >= 2:
                    ctx.wait(gp, "mm", 3 * 4 * (g - 1))
                ctx.dinc(gp.dma_start(
                    out=wbuf[g % 2][:, :].rearrange("p (c f) -> p c f", f=512),
                    in_=wqkvz.rearrange("(c p) f -> p c f", p=128)
                             [:, :, 512 * g:512 * (g + 1)]), f"dW{g % 2}")
        block.gpsimd(gpW)

        def t3(t):
            ctx.wait(t, "vH")
            for fc in range(NFC):
                g = fc // 4
                t.wait_ge(ctx.sem(f"dW{g % 2}"), 16 * (g // 2 + 1))
                if fc >= 2:
                    ctx.wait(t, "qe", fc - 1)
                wsl = wbuf[g % 2]
                fof = (fc % 4) * 128
                st = fc % 2
                for ti, (t0_, tn) in enumerate(TT3):
                    pdst = psT[st][:, 0:tn] if ti == 2 else psA[2 * st + ti][:, 0:tn]
                    for kc in range(16):
                        inst = t.matmul(pdst,
                                        wsl[:, 512 * kc + fof: 512 * kc + fof + 128],
                                        xT[:, TPAD * kc + t0_: TPAD * kc + t0_ + tn],
                                        start=(kc == 0), stop=(kc == 15))
                    ctx.inc(inst, "mm")
        block.tensor(t3)

        def a4(a):
            a = ctx.ch(a, "cA")

            def silu_of(fc):
                ctx.wait(a, "cv", fc + 1)
                if fc >= 2:
                    a.wait_ge(ctx.sem(f"dC{fc % 2}"), 16 * ((fc - 2) // 2 + 1))
                a.activation(cout[fc % 2][:, :], cacc[fc % 2][:, :], AF.Silu)
                a.mark("sl")

            for fc in range(NFC):
                st = fc % 2
                if fc < 64:
                    if fc >= 3:
                        ctx.wait(a, "cv", fc - 2)
                    dst = qkT[fc % 3]
                    for ti, (t0_, tn) in enumerate(TT3):
                        ctx.wait(a, "mm", 3 * fc + ti + 1)
                        src = psT[st][:, 0:tn] if ti == 2 else psA[2 * st + ti][:, 0:tn]
                        a.activation(dst[:, t0_:t0_ + tn], src, AF.Copy)
                    a.mark("qe")
                else:
                    zc = fc - 64
                    zi = zc % 2
                    if zc >= 2:
                        a.wait_ge(ctx.sem(f"dO{zc % 2}"), 16 * ((zc - 2) // 2 + 1))
                    dst = zout[zi]
                    ctx.wait(a, "mm", 3 * fc + 3)
                    a.activation(dst[:, 0:509], psA[2 * st][:, HALO:512], AF.Copy)
                    a.activation(dst[:, 509:1021], psA[2 * st + 1][:, 0:512], AF.Copy)
                    a.activation(dst[:, 1021:1024], psT[st][:, 0:HALO], AF.Copy)
                    a.mark("qe")
                    ctx.dinc(a.dma_start(out=zfm[128 * zc:128 * (zc + 1), :],
                                         in_=dst[:, :]), f"dO{zc % 2}")
                if 1 <= fc < 65:
                    silu_of(fc - 1)
        block.scalar(a4)

        def v2(v):
            v = ctx.ch(v, "cV")
            for fc in range(64):
                ctx.wait(v, "qe", fc + 1)
                if fc >= 2:
                    ctx.wait(v, "sl", fc - 1)
                src = qkT[fc % 3]
                acc = cacc[fc % 2]
                v.tensor_scalar_mul(acc[:, :], src[:, 0:TOWN], cw[:, 4 * fc:4 * fc + 1])
                for i in range(1, 4):
                    v.scalar_tensor_tensor(acc[:, :], src[:, i:i + TOWN],
                                           cw[:, 4 * fc + i:4 * fc + i + 1],
                                           acc[:, :], ALU.mult, ALU.add)
                v.mark("cv")
        block.vector(v2)

        def sync2(sync):
            for fc in range(64):
                ctx.wait(sync, "sl", fc + 1)
                dst, r = conv_out_of(fc)
                ctx.dinc(sync.dma_start(out=dst[128 * r:128 * (r + 1), :],
                                        in_=cout[fc % 2][:, :]), f"dC{fc % 2}")
        block.sync(sync2)

        def t4(t):
            ctx.wait(t, "qe")
            t.wait_ge(ctx.sem("dWba"), 16)
            for ti, (t0_, tn) in enumerate([(0, 512), (512, 512)]):
                for kc in range(16):
                    inst = t.matmul(
                        psA[ti][0:64, 0:tn],
                        wbab[:, 64 * kc:64 * (kc + 1)],
                        xT[:, TPAD * kc + HALO + t0_: TPAD * kc + HALO + t0_ + tn],
                        start=(kc == 0), stop=(kc == 15))
                ctx.inc(inst, "mba")
        block.tensor(t4)

        def a6(a):
            a = ctx.ch(a, "cA")
            ctx.wait(a, "mba")
            a.activation(barows[:, 0:512], psA[0][0:64, 0:512], AF.Copy)
            a.activation(barows[:, 512:1024], psA[1][0:64, 0:512], AF.Copy)
            a.activation(bout[:, :], barows[0:32, :], AF.Sigmoid)
            a.activation(negea[:, :], alogt[:, :], AF.Exp)
            a.mark("gb")
        block.scalar(a6)

        def v3(v):
            v = ctx.ch(v, "cV")
            ctx.wait(v, "gb")
            v.tensor_scalar_mul(negea[:, :], negea[:, :], -1.0)
            v.tensor_scalar_add(gw1[:, :], barows[32:64, :], dtbt[:, 0:1])
            v.mark("sp1")
        block.vector(v3)

        def a7(a):
            a = ctx.ch(a, "cA")
            ctx.wait(a, "sp1")
            a.activation(gw2[:, :], gw1[:, :], AF.Abs)
            a.activation(gw2[:, :], gw2[:, :], AF.Exp, scale=-1.0)
            a.activation(gw2[:, :], gw2[:, :], AF.Ln, bias=1.0)
            a.mark("sp2")
        block.scalar(a7)

        def v4(v):
            v = ctx.ch(v, "cV")
            ctx.wait(v, "sp2")
            v.tensor_scalar_max(gw1[:, :], gw1[:, :], 0.0)
            v.tensor_add(gw1[:, :], gw1[:, :], gw2[:, :])
            v.tensor_scalar_mul(gout[:, :], gw1[:, :], negea[:, 0:1])
            v.mark("gfin")
        block.vector(v4)

        def sync3(sync):
            ctx.wait(sync, "gb")
            ctx.dinc(sync.dma_start(out=brow[:, :], in_=bout[:, :]), "dO2")
            ctx.wait(sync, "gfin")
            ctx.dinc(sync.dma_start(out=grow[:, :], in_=gout[:, :]), "dO2")
            ctx.wait(sync, "dC0")
            ctx.wait(sync, "dC1")
            ctx.wait(sync, "dO0")
            ctx.wait(sync, "dO1")
            ctx.wait(sync, "dO2")
        block.sync(sync3)

    return nc


def launchA_inputs(inputs):
    x = inputs["L0_attn_hidden_states"]
    wb = inputs["L0_attn_W_ba"].reshape(D, 16, 4)
    wba_perm = np.concatenate([wb[:, :, 0:2].reshape(D, 32),
                               wb[:, :, 2:4].reshape(D, 32)], axis=1)
    wba_perm = np.ascontiguousarray(
        wba_perm.reshape(16, 128, 64).transpose(1, 0, 2).reshape(128, 16 * 64))
    ln1 = np.ascontiguousarray(inputs["L0_attn_ln1_w"].reshape(1, D))
    convw = np.ascontiguousarray(
        inputs["L0_attn_conv_w"].reshape(64, 128, 4).transpose(1, 0, 2).reshape(128, 256))
    dtb = np.ascontiguousarray(inputs["L0_attn_dt_bias"].reshape(32, 1))
    alog = np.ascontiguousarray(inputs["L0_attn_A_log"].reshape(32, 1))
    perm = []
    for h in range(16):
        perm += list(range(768 * h, 768 * h + 128))
    for h in range(16):
        perm += list(range(768 * h + 128, 768 * h + 256))
    for h in range(16):
        perm += list(range(768 * h + 256, 768 * h + 512))
    for h in range(16):
        perm += list(range(768 * h + 512, 768 * h + 768))
    wqkvz = np.ascontiguousarray(inputs["L0_attn_W_qkvz"][:, perm])
    maps = []
    for c in range(NCORES):
        b, hf = c // 2, c % 2
        s0 = hf * TOWN
        xsl = np.zeros((TPAD, D), np.float32)
        lo = max(0, s0 - HALO)
        xsl[HALO - (s0 - lo):HALO + TOWN] = x[b, lo:s0 + TOWN]
        maps.append(dict(xs=xsl, wqkvz=wqkvz, wba=wba_perm, ln1=ln1, convw=convw,
                         dtb=dtb, alog=alog))
    return maps



NCH = 16
CU = 16
FW = CU * 128


def build_launchC():
    nc = bass.Bass("TRN2", target_bir_lowering=False)
    kfm = nc.dram_tensor("kfm", [NCH, 128, FW], BF16, kind="ExternalInput").ap()
    qfm = nc.dram_tensor("qfm", [NCH, 128, FW], BF16, kind="ExternalInput").ap()
    ktm = nc.dram_tensor("ktm", [NCH, 128, FW], BF16, kind="ExternalInput").ap()
    qtm = nc.dram_tensor("qtm", [NCH, 128, FW], BF16, kind="ExternalInput").ap()
    vtm = nc.dram_tensor("vtm", [NCH, 128, FW], BF16, kind="ExternalInput").ap()
    ztm = nc.dram_tensor("ztm", [NCH, 128, FW], BF16, kind="ExternalInput").ap()
    grows = nc.dram_tensor("grows", [CU, 2048], F32, kind="ExternalInput").ap()
    brows = nc.dram_tensor("brows", [CU, 2048], F32, kind="ExternalInput").ap()
    gnw = nc.dram_tensor("gnw", [1, 128], F32, kind="ExternalInput").ap()
    gated = nc.dram_tensor("gated", [NCH, 128, FW], BF16, kind="ExternalOutput").ap()

    es = contextlib.ExitStack()
    with es:
        ctx = Ctx(nc, es)
        sb = lambda n, s, d: es.enter_context(nc.sbuf_tensor(n, s, d))

        sK = [sb(f"sK{i}", [128, FW], BF16) for i in range(2)]
        sQ = [sb(f"sQ{i}", [128, FW], BF16) for i in range(2)]
        sKT = [sb(f"sKT{i}", [128, FW], BF16) for i in range(2)]
        sQT = [sb(f"sQT{i}", [128, FW], BF16) for i in range(2)]
        sV = [sb(f"sV{i}", [128, FW], BF16) for i in range(2)]
        sZ = [sb(f"sZ{i}", [128, FW], BF16) for i in range(2)]
        nKK = sb("nKK", [128, FW], BF16)
        qk = sb("qk", [128, FW], BF16)
        Zb = sb("Zb", [128, FW], BF16)
        ZTb = sb("ZTb", [128, FW], BF16)
        Z2b = sb("Z2b", [128, FW], BF16)
        ZT2b = sb("ZT2b", [128, FW], BF16)
        Pb = sb("Pb", [128, FW], BF16)
        MTb = sb("MTb", [128, FW], BF16)
        argb = sb("argb", [128, FW], F32)
        scr = sb("scr", [128, FW], F32)
        expD = sb("expD", [128, FW], BF16)
        bv = sb("bv", [128, FW], BF16)
        kdel = sb("kdel", [128, FW], BF16)
        zsil = sb("zsil", [128, FW], BF16)
        Ubuf = sb("Ubuf", [128, FW], BF16)
        Rbuf = sb("Rbuf", [128, FW], BF16)
        coreTM = sb("coreTM", [128, FW], F32)
        gout = [sb(f"gout{i}", [128, FW], BF16) for i in range(2)]
        S0 = sb("S0", [128, FW], F32)
        S0b = sb("S0b", [128, FW], BF16)
        growsb = sb("growsb", [CU, 2048], F32)
        brofsb = sb("brofsb", [CU, 2048], F32)
        Gch = sb("Gch", [CU, 128], F32)
        lnb = sb("lnb", [CU, 128], F32)
        lnsk = sb("lnsk", [CU, 128], F32)
        lnsq = sb("lnsq", [CU, 128], F32)
        rowp = sb("rowp", [CU, 128], F32)
        rowq = sb("rowq", [CU, 128], F32)
        c1r = sb("c1r", [CU, 128], F32)
        darg = sb("darg", [CU, 128], F32)
        expfac = sb("expfac", [CU, 128], F32)
        expfq = sb("expfq", [CU, 128], F32)
        dexp = sb("dexp", [CU, 128], F32)
        negG = sb("negG", [CU, 128], F32)
        sskc = sb("sskc", [128, CU], F32)
        ssqc = sb("ssqc", [128, CU], F32)
        cols = sb("cols", [128, 6 * CU], F32)
        gCbc = sb("gCbc", [128, CU], F32)
        gC16 = sb("gC16", [1, CU], F32)
        ssc = sb("ssc", [128, CU], F32)
        invc = sb("invc", [128, CU], F32)
        gnwr = sb("gnwr", [1, 128], F32)
        gnwbc = sb("gnwbc", [128, 128], F32)
        identb = sb("identb", [128, 128], BF16)
        identf2 = sb("identf2", [128, 128], F32)
        onesf2 = sb("onesf2", [1, 128], F32)
        ramp = sb("ramp", [CU, 1], F32)
        SELf = sb("SELf", [CU, CU * 128], F32)
        mskUS = sb("mskUS", [128, 128], F32)
        mskLS = sb("mskLS", [128, 128], F32)
        mskUI = sb("mskUI", [128, 128], F32)

        ps0 = es.enter_context(nc.psum_tensor("ps0", [128, FW], F32))
        ps1 = es.enter_context(nc.psum_tensor("ps1", [128, FW], F32))

        block = es.enter_context(nc.Block())
        C1, RP, BC, EF, EQ, DC = range(6)
        col = lambda which, u: cols[:, which * CU + u: which * CU + u + 1]

        def bcAP(tile):
            return tile[:, :].unsqueeze(1).broadcast_to([128, CU, 128])

        def sy0(sync):
            ctx.dinc(sync.dma_start(out=growsb[:, :], in_=grows[:, :]), "dS")
            ctx.dinc(sync.dma_start(out=brofsb[:, :], in_=brows[:, :]), "dS")
            ctx.dinc(sync.dma_start(out=gnwr[:, :], in_=gnw[:, :]), "dS")
        block.sync(sy0)

        def gp0(gp):
            g = ctx.ch(gp, "cG")
            g.memset(onesf2[:, :], 1.0)
            make_identity(g, identb[:, :])
            make_identity(g, identf2[:, :])
            g.memset(mskUS[:, :], 0.0)
            g.affine_select(out=mskUS[:, :], in_=mskUS[:, :], compare_op=ALU.is_gt,
                            fill=-200.0, base=0, pattern=[[1, 128]],
                            channel_multiplier=-1)
            g.memset(mskLS[:, :], 0.0)
            g.affine_select(out=mskLS[:, :], in_=mskLS[:, :], compare_op=ALU.is_gt,
                            fill=-200.0, base=0, pattern=[[-1, 128]],
                            channel_multiplier=1)
            g.memset(mskUI[:, :], 0.0)
            g.affine_select(out=mskUI[:, :], in_=mskUI[:, :], compare_op=ALU.is_ge,
                            fill=-200.0, base=0, pattern=[[1, 128]],
                            channel_multiplier=-1)
            g.iota(ramp[:, :], pattern=[[0, 1]], channel_multiplier=1,
                   allow_small_or_imprecise_dtypes=True)
            g.mark("gI")
        block.gpsimd(gp0)

        def v0(v):
            v = ctx.ch(v, "cV")
            ctx.wait(v, "gI")
            for u in range(CU):
                v.tensor_scalar(SELf[:, 128 * u:128 * (u + 1)],
                                ramp[:, 0:1].broadcast_to([CU, 128]),
                                float(u), None, ALU.is_equal)
            v.memset(S0[:, :], 0.0)
            v.memset(S0b[:, :], 0.0)
            v.mark("vI")
        block.vector(v0)

        def t0(t):
            ctx.wait(t, "dS")
            ctx.wait(t, "vI")
            ctx.inc(t.matmul(ps0[:, 0:128], onesf2[:, :], gnwr[:, :],
                             start=True, stop=True), "tI")
        block.tensor(t0)

        def a0(a):
            a = ctx.ch(a, "cA")
            ctx.wait(a, "tI")
            a.activation(gnwbc[:, :], ps0[:, 0:128], AF.Copy)
            a.mark("aI")
        block.scalar(a0)

        def syL(sync):
            for c in range(NCH):
                if c >= 2:
                    ctx.wait(sync, "uc", c - 1)
                pr = c % 2
                for nm, dst, srct in (("k", sK, kfm), ("q", sQ, qfm),
                                      ("kt", sKT, ktm), ("qt", sQT, qtm),
                                      ("v", sV, vtm), ("z", sZ, ztm)):
                    ctx.dinc(sync.dma_start(out=dst[pr][:, :], in_=srct[c]),
                             f"dc{pr}")
        block.sync(syL)

        for c in range(NCH):
            pr = c % 2
            NLOAD = 16 * 6 * (c // 2 + 1)

            def vp(v, c=c, pr=pr, NLOAD=NLOAD):
                v = ctx.ch(v, "cV")
                if c == 0:
                    ctx.wait(v, "aI")
                v.wait_ge(ctx.sem(f"dc{pr}"), NLOAD)
                v.tensor_tensor_scan(Gch[:, :], growsb[:, 128 * c:128 * (c + 1)],
                                     growsb[:, 128 * c:128 * (c + 1)], 0.0,
                                     ALU.add, ALU.bypass)
                v.mark("vG")
            block.vector(vp)

            def ap(a, c=c, pr=pr, NLOAD=NLOAD):
                a = ctx.ch(a, "cA")
                a.wait_ge(ctx.sem(f"dc{pr}"), NLOAD)
                if c >= 1:
                    ctx.wait(a, "sqdone", c)
                a.activation(scr[:, :], sKT[pr][:, :], AF.Square)
                a.mark("aK2")
                a.activation(lnb[:, :], brofsb[:, 128 * c:128 * (c + 1)], AF.Ln)
                a.mark("aLnb")
            block.scalar(ap)

            def vp2(v, c=c, pr=pr):
                v = ctx.ch(v, "cV")
                ctx.wait(v, "aK2")
                v.tensor_reduce(sskc[:, :],
                                scr[:, :].rearrange("p (u f) -> p u f", f=128),
                                AX.X, ALU.add)
                v.mark("vK2")
            block.vector(vp2)

            def ap2(a, c=c, pr=pr):
                a = ctx.ch(a, "cA")
                ctx.wait(a, "vK2")
                a.activation(scr[:, :], sQT[pr][:, :], AF.Square)
                a.mark("aQ2")
            block.scalar(ap2)

            def vp3(v, c=c, pr=pr):
                v = ctx.ch(v, "cV")
                ctx.wait(v, "aQ2")
                v.tensor_reduce(ssqc[:, :],
                                scr[:, :].rearrange("p (u f) -> p u f", f=128),
                                AX.X, ALU.add)
                v.tensor_scalar_add(sskc[:, :], sskc[:, :], 1e-6)
                v.tensor_scalar_add(ssqc[:, :], ssqc[:, :], 1e-6)
                v.mark("vQ2")
            block.vector(vp3)

            def tp(t, c=c):
                ctx.wait(t, "vQ2", ctx.cnt["vQ2"])
                if c >= 1:
                    ctx.wait(t, "aEvRows", c)
                ctx.inc(t.transpose(ps1[0:CU, 0:128], sskc[:, :], identf2[:, :]),
                        "tRow")
                ctx.inc(t.transpose(ps1[0:CU, 128:256], ssqc[:, :], identf2[:, :]),
                        "tRow")
            block.tensor(tp)

            def ap3(a, c=c):
                a = ctx.ch(a, "cA")
                ctx.wait(a, "tRow")
                a.activation(lnsk[:, :], ps1[0:CU, 0:128], AF.Ln)
                a.activation(lnsq[:, :], ps1[0:CU, 128:256], AF.Ln)
                a.mark("aEvRows")
            block.scalar(ap3)

            def vp4(v, c=c):
                v = ctx.ch(v, "cV")
                ctx.wait(v, "aEvRows")
                ctx.wait(v, "vG", ctx.cnt["vG"])
                ctx.wait(v, "aLnb")
                v.tensor_add(rowp[:, :], Gch[:, :], lnb[:, :])
                v.scalar_tensor_tensor(rowp[:, :], lnsk[:, :], -0.5, rowp[:, :],
                                       ALU.mult, ALU.add)
                v.tensor_scalar_mul(negG[:, :], Gch[:, :], -1.0)
                v.scalar_tensor_tensor(c1r[:, :], lnsk[:, :], -0.5, negG[:, :],
                                       ALU.mult, ALU.add)
                v.scalar_tensor_tensor(rowq[:, :], lnsq[:, :], -0.5, Gch[:, :],
                                       ALU.mult, ALU.add)
                v.tensor_scalar_add(rowq[:, :], rowq[:, :], float(-0.5 * np.log(128.0)))
                v.tensor_scalar_add(darg[:, :], c1r[:, :], Gch[:, 127:128])
                v.mark("vRows")
            block.vector(vp4)

            def ap4(a, c=c):
                a = ctx.ch(a, "cA")
                ctx.wait(a, "vRows")
                a.activation(expfac[:, :], rowp[:, :], AF.Exp)
                a.activation(expfq[:, :], rowq[:, :], AF.Exp)
                a.activation(dexp[:, :], darg[:, :], AF.Exp)
                a.mark("aRows")
            block.scalar(ap4)

            def tp2(t, c=c):
                ctx.wait(t, "aRows")
                if c >= 1:
                    ctx.wait(t, "aEvCols", c)
                for i, src in enumerate([c1r, rowp, None, expfac, expfq, dexp]):
                    if src is None:
                        src = brofsb[:, 128 * c:128 * (c + 1)]
                        ctx.inc(t.transpose(ps1[0:128, 16 * i:16 * (i + 1)],
                                            src, identf2[0:CU, 0:CU]), "tCol")
                    else:
                        ctx.inc(t.transpose(ps1[0:128, 16 * i:16 * (i + 1)],
                                            src[:, :], identf2[0:CU, 0:CU]), "tCol")
                ctx.inc(t.transpose(ps1[0:1, 96:112], Gch[:, 127:128],
                                    identf2[0:CU, 0:CU]), "tCol")
            block.tensor(tp2)

            def ap5(a, c=c):
                a = ctx.ch(a, "cA")
                ctx.wait(a, "tCol")
                a.activation(cols[:, 0:48], ps1[0:128, 0:48], AF.Copy)
                a.activation(cols[:, 48:64], ps1[0:128, 48:64], AF.Copy, scale=-1.0)
                a.activation(cols[:, 64:96], ps1[0:128, 64:96], AF.Copy)
                a.activation(gC16[:, :], ps1[0:1, 96:112], AF.Exp)
                a.mark("aEvCols")
            block.scalar(ap5)

            def tp3(t, c=c):
                ctx.wait(t, "aEvCols")
                ctx.inc(t.matmul(ps1[:, 0:16], onesf2[:, :], gC16[:, :],
                                 start=True, stop=True), "tGC")
            block.tensor(tp3)

            def ap6(a, c=c):
                a = ctx.ch(a, "cA")
                ctx.wait(a, "tGC")
                a.activation(gCbc[:, :], ps1[:, 0:16], AF.Copy)
                a.mark("aGC")
            block.scalar(ap6)

            def tm1(t, c=c, pr=pr, NLOAD=NLOAD):
                t.wait_ge(ctx.sem(f"dc{pr}"), NLOAD)
                if c >= 1:
                    ctx.wait(t, "evKK", c)
                    ctx.wait(t, "vS", ctx.cnt["vS"])
                for u in range(CU):
                    ctx.inc(t.matmul(ps0[:, 128 * u:128 * (u + 1)],
                                     sK[pr][:, 128 * u:128 * (u + 1)],
                                     sK[pr][:, 128 * u:128 * (u + 1)],
                                     start=True, stop=True), "mKK")
            block.tensor(tm1)

            def aev1(a, c=c):
                a = ctx.ch(a, "cA")
                ctx.wait(a, "mKK")
                a.activation(nKK[:, :], ps0[:, :], AF.Copy, scale=-1.0)
                a.mark("evKK1")
            block.scalar(aev1)

            def tm2(t, c=c, pr=pr):
                ctx.wait(t, "evKK1")
                for u in range(CU):
                    ctx.inc(t.matmul(ps0[:, 128 * u:128 * (u + 1)],
                                     sK[pr][:, 128 * u:128 * (u + 1)],
                                     sQ[pr][:, 128 * u:128 * (u + 1)],
                                     start=True, stop=True), "mQK")
            block.tensor(tm2)

            def aev2(a, c=c):
                a = ctx.ch(a, "cA")
                ctx.wait(a, "mQK")
                a.activation(qk[:, :], ps0[:, :], AF.Copy)
                a.mark("evKK")
            block.scalar(aev2)

            def build_DM(rowsrc, colidx, msk, kkbuf, outbuf, tag, c=c):
                def tsel(t, rowsrc=rowsrc, tag=tag):
                    ctx.wait(t, "aRows", ctx.cnt["aRows"])
                    ctx.wait(t, "evKK", ctx.cnt["evKK"])
                    ctx.wait(t, f"bd_{tag}", ctx.cnt[f"bd_{tag}"])
                    for tg in ("NT", "N", "MT"):
                        ctx.wait(t, f"varg_{tg}", ctx.cnt[f"varg_{tg}"])
                    for u in range(CU):
                        ctx.inc(t.matmul(ps0[:, 128 * u:128 * (u + 1)],
                                         SELf[:, 128 * u:128 * (u + 1)],
                                         rowsrc[:, :], start=True, stop=True),
                                f"msel_{tag}")
                block.tensor(tsel)

                def vsel(v, colidx=colidx, msk=msk, tag=tag):
                    v = ctx.ch(v, "cV")
                    ctx.wait(v, f"msel_{tag}")
                    for u in range(CU):
                        v.tensor_scalar_add(argb[:, 128 * u:128 * (u + 1)],
                                            ps0[:, 128 * u:128 * (u + 1)],
                                            col(colidx, u))
                    v.tensor_tensor(argb[:, :], argb[:, :], bcAP(msk), ALU.min)
                    v.mark(f"varg_{tag}")
                block.vector(vsel)

                def asel(a, tag=tag):
                    a = ctx.ch(a, "cA")
                    ctx.wait(a, f"varg_{tag}")
                    a.activation(expD[:, :], argb[:, :], AF.Exp)
                    a.mark(f"aexp_{tag}")
                block.scalar(asel)

                def vsel2(v, kkbuf=kkbuf, outbuf=outbuf, tag=tag):
                    v = ctx.ch(v, "cV")
                    ctx.wait(v, f"aexp_{tag}")
                    v.tensor_mul(outbuf[:, :], expD[:, :], kkbuf[:, :])
                    v.mark(f"bd_{tag}")
                block.vector(vsel2)

            for tg in ("NT", "N", "MT"):
                ctx.sem(f"bd_{tg}"); ctx.sem(f"varg_{tg}"); ctx.sem(f"aexp_{tg}")
                ctx.sem(f"msel_{tg}")
            build_DM(rowp, C1, mskUS, nKK, Zb, "NT")
            build_DM(c1r, RP, mskLS, nKK, ZTb, "N")
            build_DM(rowq, C1, mskUI, qk, MTb, "MT")

            def vinv0(v, c=c):
                v = ctx.ch(v, "cV")
                ctx.wait(v, "bd_NT")
                v.tensor_tensor(Pb[:, :], Zb[:, :], bcAP(identb), ALU.add)
                v.mark("vP0")
            block.vector(vinv0)

            for k in range(1, 7):
                def tinv(t, k=k, c=c):
                    if k == 1:
                        ctx.wait(t, "bd_N", ctx.cnt["bd_N"])
                        ctx.wait(t, "vP0", ctx.cnt["vP0"])
                    else:
                        ctx.wait(t, "evZ", ctx.cnt["evZ"])
                        ctx.wait(t, "vPup", ctx.cnt["vPup"])
                    if k < 7:
                        for u in range(CU):
                            ctx.inc(t.matmul(ps0[:, 128 * u:128 * (u + 1)],
                                             ZTb[:, 128 * u:128 * (u + 1)],
                                             Zb[:, 128 * u:128 * (u + 1)],
                                             start=True, stop=True), "mZ2")
                        for u in range(CU):
                            ctx.inc(t.matmul(ps1[:, 128 * u:128 * (u + 1)],
                                             Zb[:, 128 * u:128 * (u + 1)],
                                             ZTb[:, 128 * u:128 * (u + 1)],
                                             start=True, stop=True), "mZT2")
                block.tensor(tinv)

                def aev3(a, k=k, c=c):
                    a = ctx.ch(a, "cA")
                    ctx.wait(a, "mZ2")
                    a.activation(Z2b[:, :], ps0[:, :], AF.Copy)
                    ctx.wait(a, "mZT2")
                    a.activation(ZT2b[:, :], ps1[:, :], AF.Copy)
                    a.mark("evZ")
                block.scalar(aev3)

                def tinv2(t, k=k, c=c):
                    ctx.wait(t, "evZ", ctx.cnt["evZ"])
                    for u in range(CU):
                        ctx.inc(t.matmul(ps0[:, 128 * u:128 * (u + 1)],
                                         ZT2b[:, 128 * u:128 * (u + 1)],
                                         Pb[:, 128 * u:128 * (u + 1)],
                                         start=True, stop=True), "mPup")
                block.tensor(tinv2)

                def vpup(v, k=k, c=c):
                    v = ctx.ch(v, "cV")
                    ctx.wait(v, "mPup")
                    v.tensor_add(Pb[:, :], Pb[:, :], ps0[:, :])
                    v.tensor_copy(Zb[:, :], Z2b[:, :])
                    v.tensor_copy(ZTb[:, :], ZT2b[:, :])
                    v.mark("vPup")
                block.vector(vpup)

            def vprep(v, c=c, pr=pr):
                v = ctx.ch(v, "cV")
                ctx.wait(v, "aEvCols", ctx.cnt["aEvCols"])
                for u in range(CU):
                    v.tensor_scalar_mul(bv[:, 128 * u:128 * (u + 1)],
                                        sV[pr][:, 128 * u:128 * (u + 1)],
                                        col(BC, u))
                    v.tensor_scalar_mul(kdel[:, 128 * u:128 * (u + 1)],
                                        sKT[pr][:, 128 * u:128 * (u + 1)],
                                        col(DC, u))
                v.mark("vBV")
            block.vector(vprep)

            def aprep(a, c=c, pr=pr):
                a = ctx.ch(a, "cA")
                ctx.wait(a, "vBV")
                a.activation(zsil[:, :], sZ[pr][:, :], AF.Silu)
                a.mark("aZ")
            block.scalar(aprep)

            def ts1(t, c=c, pr=pr):
                ctx.wait(t, "vPup", ctx.cnt["vPup"])
                ctx.wait(t, "vBV", ctx.cnt["vBV"])
                if c >= 1:
                    ctx.wait(t, "sqdone", c)
                for u in range(CU):
                    ctx.inc(t.matmul(ps0[:, 128 * u:128 * (u + 1)],
                                     sK[pr][:, 128 * u:128 * (u + 1)],
                                     S0b[:, 128 * u:128 * (u + 1)],
                                     start=True, stop=True), "mW0")
            block.tensor(ts1)

            def vs1(v, c=c):
                v = ctx.ch(v, "cV")
                ctx.wait(v, "mW0")
                for u in range(CU):
                    v.scalar_tensor_tensor(Rbuf[:, 128 * u:128 * (u + 1)],
                                           ps0[:, 128 * u:128 * (u + 1)],
                                           col(EF, u),
                                           bv[:, 128 * u:128 * (u + 1)],
                                           ALU.mult, ALU.add)
                v.mark("vR")
            block.vector(vs1)

            def ts2(t, c=c):
                ctx.wait(t, "vR", ctx.cnt["vR"])
                for u in range(CU):
                    ctx.inc(t.matmul(ps1[:, 128 * u:128 * (u + 1)],
                                     Pb[:, 128 * u:128 * (u + 1)],
                                     Rbuf[:, 128 * u:128 * (u + 1)],
                                     start=True, stop=True), "mU")
            block.tensor(ts2)

            def as1(a, c=c):
                a = ctx.ch(a, "cA")
                ctx.wait(a, "mU")
                a.activation(Ubuf[:, :], ps1[:, :], AF.Copy)
                a.mark("aU")
            block.scalar(as1)

            def ts3(t, c=c, pr=pr):
                ctx.wait(t, "aU", ctx.cnt["aU"])
                for u in range(CU):
                    ctx.inc(t.matmul(ps0[:, 128 * u:128 * (u + 1)],
                                     MTb[:, 128 * u:128 * (u + 1)],
                                     Ubuf[:, 128 * u:128 * (u + 1)],
                                     start=True, stop=True), "mOM")
                for u in range(CU):
                    ctx.inc(t.matmul(ps1[:, 128 * u:128 * (u + 1)],
                                     sQ[pr][:, 128 * u:128 * (u + 1)],
                                     S0b[:, 128 * u:128 * (u + 1)],
                                     start=True, stop=True), "mOQ")
            block.tensor(ts3)

            def as2(a, c=c):
                a = ctx.ch(a, "cA")
                ctx.wait(a, "mOQ")
                for u in range(CU):
                    a.activation(scr[:, 128 * u:128 * (u + 1)],
                                 ps1[:, 128 * u:128 * (u + 1)], AF.Copy,
                                 scale=col(EQ, u))
                a.mark("aOQ")
            block.scalar(as2)

            def vs2(v, c=c):
                v = ctx.ch(v, "cV")
                ctx.wait(v, "aOQ")
                ctx.wait(v, "mOM")
                v.tensor_add(coreTM[:, :], scr[:, :], ps0[:, :])
                v.mark("vCore")
            block.vector(vs2)

            def as3(a, c=c):
                a = ctx.ch(a, "cA")
                ctx.wait(a, "vCore")
                a.activation(scr[:, :], coreTM[:, :], AF.Square)
                a.mark("aC2")
            block.scalar(as3)

            def vs3(v, c=c):
                v = ctx.ch(v, "cV")
                ctx.wait(v, "aC2")
                v.tensor_reduce(ssc[:, :],
                                scr[:, :].rearrange("p (u f) -> p u f", f=128),
                                AX.X, ALU.add)
                v.tensor_scalar(ssc[:, :], ssc[:, :], 1.0 / 128.0, 1e-6,
                                ALU.mult, ALU.add)
                v.mark("vSS")
            block.vector(vs3)

            def as4(a, c=c):
                a = ctx.ch(a, "cA")
                ctx.wait(a, "vSS")
                a.activation(invc[:, :], ssc[:, :], AF.Ln)
                a.activation(invc[:, :], invc[:, :], AF.Exp, scale=-0.5)
                a.mark("aInvC")
            block.scalar(as4)

            def vs4(v, c=c):
                v = ctx.ch(v, "cV")
                ctx.wait(v, "aInvC")
                ctx.wait(v, "aZ", ctx.cnt["aZ"])
                v.tensor_mul(coreTM[:, :], coreTM[:, :],
                             invc[:, :].unsqueeze(2).broadcast_to([128, CU, 128]))
                v.tensor_mul(coreTM[:, :], coreTM[:, :], bcAP(gnwbc))
                if c >= 2:
                    v.wait_ge(ctx.sem(f"dGo{c % 2}"), 16 * ((c - 2) // 2 + 1))
                v.tensor_mul(gout[c % 2][:, :], coreTM[:, :], zsil[:, :])
                v.mark("vGate")
                v.mark("sqdone_pre")
            block.vector(vs4)

            def gpO(gp, c=c):
                ctx.wait(gp, "vGate", ctx.cnt["vGate"])
                ctx.dinc(gp.dma_start(out=gated[c], in_=gout[c % 2][:, :]),
                         f"dGo{c % 2}")
            block.gpsimd(gpO)

            def ts4(t, c=c):
                ctx.wait(t, "aU", ctx.cnt["aU"])
                ctx.wait(t, "vCore", ctx.cnt["vCore"])
                ctx.wait(t, "vBV", ctx.cnt["vBV"])
                for u in range(CU):
                    ctx.inc(t.matmul(ps0[:, 128 * u:128 * (u + 1)],
                                     kdel[:, 128 * u:128 * (u + 1)],
                                     Ubuf[:, 128 * u:128 * (u + 1)],
                                     start=True, stop=True), "mS")
            block.tensor(ts4)

            def vs5(v, c=c):
                v = ctx.ch(v, "cV")
                ctx.wait(v, "mS")
                ctx.wait(v, "aGC", ctx.cnt["aGC"])
                for u in range(CU):
                    v.scalar_tensor_tensor(S0[:, 128 * u:128 * (u + 1)],
                                           S0[:, 128 * u:128 * (u + 1)],
                                           gCbc[:, u:u + 1],
                                           ps0[:, 128 * u:128 * (u + 1)],
                                           ALU.mult, ALU.add)
                v.mark("vS")
            block.vector(vs5)

            def as5(a, c=c):
                a = ctx.ch(a, "cA")
                ctx.wait(a, "vS")
                a.activation(S0b[:, :], S0[:, :], AF.Copy)
                a.mark("sqdone")
            block.scalar(as5)

            def vdone(v, c=c):
                v = ctx.ch(v, "cV")
                v.mark("uc")
            block.vector(vdone)

        def gpF(gp):
            ctx.wait(gp, "dGo0")
            ctx.wait(gp, "dGo1")
        block.gpsimd(gpF)

    return nc


def launchC_inputs(inputs, ares):
    gnw_arr = np.ascontiguousarray(
        inputs["L0_attn_gated_norm_w"].reshape(1, 128).astype(np.float32))
    maps = []
    per_b = []
    for b in range(4):
        r0, r1 = ares[2 * b], ares[2 * b + 1]
        cq = np.concatenate([r0["convq"], r1["convq"]], axis=1)
        ck = np.concatenate([r0["convk"], r1["convk"]], axis=1)
        cv = np.concatenate([r0["convv"], r1["convv"]], axis=1)
        cz = np.concatenate([r0["zfm"], r1["zfm"]], axis=1)
        gg = np.concatenate([r0["grow"], r1["grow"]], axis=1)
        bb = np.concatenate([r0["brow"], r1["brow"]], axis=1)
        per_b.append((cq, ck, cv, cz, gg, bb))
    for core in range(NCORES):
        units = [(u // 32, u % 32) for u in range(16 * core, 16 * core + 16)]
        kf = np.zeros((NCH, 128, FW), BF)
        qf = np.zeros((NCH, 128, FW), BF)
        kt = np.zeros((NCH, 128, FW), BF)
        qt = np.zeros((NCH, 128, FW), BF)
        vt = np.zeros((NCH, 128, FW), BF)
        zt = np.zeros((NCH, 128, FW), BF)
        gr = np.zeros((CU, 2048), np.float32)
        br = np.zeros((CU, 2048), np.float32)
        for ui, (b, hv) in enumerate(units):
            cq, ck, cv, cz, gg, bb = per_b[b]
            hk = hv // 2
            qh = cq[128 * hk:128 * (hk + 1)]
            kh = ck[128 * hk:128 * (hk + 1)]
            vh = cv[128 * hv:128 * (hv + 1)]
            zh = cz[128 * hv:128 * (hv + 1)]
            qr = qh.reshape(128, NCH, 128)
            kr = kh.reshape(128, NCH, 128)
            for c in range(NCH):
                kf[c, :, 128 * ui:128 * (ui + 1)] = kr[:, c, :]
                qf[c, :, 128 * ui:128 * (ui + 1)] = qr[:, c, :]
                kt[c, :, 128 * ui:128 * (ui + 1)] = kr[:, c, :].T
                qt[c, :, 128 * ui:128 * (ui + 1)] = qr[:, c, :].T
                vt[c, :, 128 * ui:128 * (ui + 1)] = vh[:, 128 * c:128 * (c + 1)].T
                zt[c, :, 128 * ui:128 * (ui + 1)] = zh[:, 128 * c:128 * (c + 1)].T
            gr[ui] = gg[hv]
            br[ui] = bb[hv]
        maps.append(dict(kfm=kf, qfm=qf, ktm=kt, qtm=qt, vtm=vt, ztm=zt,
                         grows=gr, brows=br, gnw=gnw_arr))
    return maps



def build_launchD():
    nc = bass.Bass("TRN2", target_bir_lowering=False)
    gatedT = nc.dram_tensor("gatedT", [4096, TOWN], BF16, kind="ExternalInput").ap()
    wout = nc.dram_tensor("wout", [4096, D], F32, kind="ExternalInput").ap()
    xown = nc.dram_tensor("xown", [TOWN, D], F32, kind="ExternalInput").ap()
    ln2 = nc.dram_tensor("ln2", [1, D], F32, kind="ExternalInput").ap()
    wgate = nc.dram_tensor("wgate", [D, 5120], F32, kind="ExternalInput").ap()
    wup = nc.dram_tensor("wup", [D, 5120], F32, kind="ExternalInput").ap()
    wdown = nc.dram_tensor("wdown", [5120, D], F32, kind="ExternalInput").ap()
    x1o = nc.dram_tensor("x1o", [TOWN, D], F32, kind="ExternalOutput").ap()
    h2bf = nc.dram_tensor("h2bf", [TOWN, D], BF16).ap()

    NIC = 40
    es = contextlib.ExitStack()
    with es:
        ctx = Ctx(nc, es)
        sb = lambda n, s, d: es.enter_context(nc.sbuf_tensor(n, s, d))

        hmid = sb("hmid", [128, 8 * D], F32)
        h2tb = sb("h2tb", [128, D], BF16)
        ln2row = sb("ln2row", [1, D], F32)
        ln2bc = sb("ln2bc", [128, D], F32)
        onesf = sb("onesf", [1, 128], F32)
        invt = sb("invt", [128, 8], F32)
        sumt = sb("sumt", [128, 8], F32)
        msq = sb("msq", [128, 8], F32)
        scr2 = sb("scr2", [128, D], F32)
        h2T = sb("h2T", [128, 16 * TOWN], BF16)

        block = es.enter_context(nc.Block())

        px = contextlib.ExitStack()
        xst = [px.enter_context(nc.sbuf_tensor(f"xst{i}", [128, D], F32))
               for i in range(2)]
        p1 = contextlib.ExitStack()
        gTc = [p1.enter_context(nc.sbuf_tensor(f"gTc{i}", [128, TOWN], BF16))
               for i in range(2)]
        wob = [p1.enter_context(nc.sbuf_tensor("wob0", [128, 32 * 512], BF16))]
        wob = [wob[0], wob[0]]
        p1b = contextlib.ExitStack()
        psB2 = p1b.enter_context(nc.psum_tensor("psB2", [128, 512], F32))

        def sy0(sync):
            ctx.dinc(sync.dma_start(out=ln2row[:, :], in_=ln2[:, :]), "dS")
        block.sync(sy0)

        def syG(sync):
            for ft in range(4):
                for vc in range(32):
                    n = ft * 32 + vc
                    if n >= 2:
                        m = n - 2
                        if m % 32 == 31:
                            ctx.wait(sync, "mWo", m // 32 + 1)
                        else:
                            ctx.wait(sync, "mWoC", m + 1 - m // 32)
                    ctx.dinc(sync.dma_start(out=gTc[n % 2][:, :],
                                            in_=gatedT[128 * vc:128 * (vc + 1), :]),
                             f"dgT{n % 2}")
        block.sync(syG)

        def syX(sync):
            for i in range(8):
                if i >= 2:
                    ctx.wait(sync, "xadd", i - 1)
                ctx.dinc(sync.dma_start(out=xst[i % 2][:, :],
                                        in_=xown[128 * i:128 * (i + 1), :]),
                         f"dX{i % 2}")
        block.sync(syX)

        def gp0(gp):
            g = ctx.ch(gp, "cG")
            g.memset(onesf[:, :], 1.0)
            g.mark("gE")
            for ft in range(4):
                if ft >= 1:
                    ctx.wait(gp, "mWo", ft)
                ctx.dinc(gp.dma_start(
                    out=wob[0][:, :].rearrange("p (c f) -> p c f", f=512),
                    in_=wout.rearrange("(c p) f -> p c f", p=128)
                            [:, :, 512 * ft:512 * (ft + 1)]), "dWoS")
        block.gpsimd(gp0)

        def t1(t):
            ctx.wait(t, "dS")
            ctx.wait(t, "gE")
            for i in range(4):
                if i >= 1:
                    ctx.wait(t, "aBc2", i)
                ctx.inc(t.matmul(psB2[:, :], onesf[:, :],
                                 ln2row[:, 512 * i:512 * (i + 1)],
                                 start=True, stop=True), "tBc2")
        block.tensor(t1)

        def a1(a):
            a = ctx.ch(a, "cA")
            for i in range(4):
                ctx.wait(a, "tBc2", i + 1)
                a.activation(ln2bc[:, 512 * i:512 * (i + 1)], psB2[:, :], AF.Copy)
                a.mark("aBc2")
        block.scalar(a1)
        p1b.close()

        psD = [p1.enter_context(nc.psum_tensor(f"psD{i}", [128, 512], F32))
               for i in range(8)]

        def t2(t):
            ctx.wait(t, "aBc2")
            for ft in range(4):
                t.wait_ge(ctx.sem("dWoS"), 16 * (ft + 1))
                if ft >= 1:
                    ctx.wait(t, "evD", 8 * ft)
                for vc in range(32):
                    n = ft * 32 + vc
                    t.wait_ge(ctx.sem(f"dgT{n % 2}"), 16 * (n // 2 + 1))
                    for tc in range(8):
                        inst = t.matmul(
                            psD[tc][:, :],
                            gTc[n % 2][:, 128 * tc:128 * (tc + 1)],
                            wob[ft % 2][:, 512 * vc:512 * (vc + 1)],
                            start=(vc == 0), stop=(vc == 31))
                    if vc == 31:
                        ctx.inc(inst, "mWo")
                    else:
                        ctx.inc(inst, "mWoC")
        block.tensor(t2)

        def aev(a):
            a = ctx.ch(a, "cA")
            for ft in range(4):
                ctx.wait(a, "mWo", ft + 1)
                for tc in range(8):
                    a.activation(hmid[:, D * tc + 512 * ft: D * tc + 512 * (ft + 1)],
                                 psD[tc][:, :], AF.Copy)
                    a.mark("evD")
        block.scalar(aev)

        def v1x(v):
            v = ctx.ch(v, "cV")
            for tc in range(8):
                ctx.wait(v, "evD", 25 + tc)
                v.wait_ge(ctx.sem(f"dX{tc % 2}"), 16 * (tc // 2 + 1))
                v.tensor_add(hmid[:, D * tc:D * (tc + 1)],
                             hmid[:, D * tc:D * (tc + 1)], xst[tc % 2][:, :])
                v.mark("xadd")
        block.vector(v1x)

        p1.close()

        def a2(a):
            a = ctx.ch(a, "cA")
            ctx.wait(a, "xadd")
            for tc in range(8):
                a.activation(scr2[:, :], hmid[:, D * tc:D * (tc + 1)], AF.Square,
                             accum_out=sumt[:, tc:tc + 1])
            a.mark("aSq2")
        block.scalar(a2)

        def v2(v):
            v = ctx.ch(v, "cV")
            ctx.wait(v, "aSq2")
            v.tensor_scalar(invt[:, :], sumt[:, :], 1.0 / D, 1e-6, ALU.mult, ALU.add)
            v.mark("vInv2")
        block.vector(v2)

        def a3(a):
            a = ctx.ch(a, "cA")
            ctx.wait(a, "vInv2")
            a.activation(invt[:, :], invt[:, :], AF.Ln)
            a.activation(invt[:, :], invt[:, :], AF.Exp, scale=-0.5)
            a.mark("aInv2")
        block.scalar(a3)

        def v3(v):
            v = ctx.ch(v, "cV")
            ctx.wait(v, "aInv2")
            ctx.wait(v, "aBc2")
            for tc in range(8):
                if tc >= 1:
                    v.wait_ge(ctx.sem("dH2"), 16 * tc)
                v.scalar_tensor_tensor(h2tb[:, :], hmid[:, D * tc:D * (tc + 1)],
                                       invt[:, tc:tc + 1], ln2bc[:, :],
                                       ALU.mult, ALU.mult)
                v.mark("vH2")
        block.vector(v3)

        def sy2(sync):
            for tc in range(8):
                ctx.wait(sync, "vH2", tc + 1)
                ctx.dinc(sync.dma_start(out=h2bf[128 * tc:128 * (tc + 1), :],
                                        in_=h2tb[:, :]), "dH2")
            ctx.wait(sync, "dH2")
            for j in range(16):
                ctx.dinc(sync.dma_start(out=h2T[:, TOWN * j:TOWN * (j + 1)],
                                        in_=h2bf[:, 128 * j:128 * (j + 1)],
                                        transpose=True), "dH2T")
        block.sync(sy2)

        def v4(v):
            v = ctx.ch(v, "cV")
            ctx.wait(v, "vH2")
            for tc in range(8):
                v.wait_ge(ctx.sem(f"dX2{tc % 2}"), 16 * (tc // 2 + 1))
                v.tensor_add(hmid[:, D * tc:D * (tc + 1)],
                             hmid[:, D * tc:D * (tc + 1)], xst[tc % 2][:, :])
                v.mark("vHX")
        block.vector(v4)

        def sy3(sync):
            ctx.wait(sync, "vH2")
            for i in range(8):
                if i >= 2:
                    ctx.wait(sync, "vHX", i - 1)
                ctx.dinc(sync.dma_start(out=xst[i % 2][:, :],
                                        in_=xown[128 * i:128 * (i + 1), :]),
                         f"dX2{i % 2}")
        block.sync(sy3)

        px.close()
        p3 = contextlib.ExitStack()
        interF = p3.enter_context(nc.sbuf_tensor("interF", [128, NIC * 512], BF16))
        _wg = p3.enter_context(nc.sbuf_tensor("wgb0", [128, 16 * 256], BF16))
        _wu = p3.enter_context(nc.sbuf_tensor("wub0", [128, 16 * 256], BF16))
        wgb = [_wg, _wg]
        wub = [_wu, _wu]
        wdb = [p3.enter_context(nc.sbuf_tensor(f"wdb{i}", [128, 8 * 512], BF16))
               for i in range(2)]
        silub = p3.enter_context(nc.sbuf_tensor("silub", [128, 512], BF16))
        psG = [p3.enter_context(nc.psum_tensor(f"psG{i}", [128, 512], F32))
               for i in range(2)]
        psU = [p3.enter_context(nc.psum_tensor(f"psU{i}", [128, 512], F32))
               for i in range(2)]
        psY = [p3.enter_context(nc.psum_tensor(f"psY{i}", [128, 512], F32))
               for i in range(4)]

        NG = NIC // 2
        for s_ in ("evGU", "aSilu", "mGU", "evY", "mDn"):
            ctx.sem(s_)

        for th in range(2):
            def gpW(gp, th=th):
                if th == 0:
                    ctx.wait(gp, "mWo")
                for grp in range(NG):
                    n = th * NG + grp
                    if n >= 1:
                        ctx.wait(gp, "mGU", 4 * n)
                    sl = slice(256 * grp, 256 * (grp + 1))
                    ctx.dinc(gp.dma_start(
                        out=wgb[0][:, :].rearrange("p (c f) -> p c f", f=256),
                        in_=wgate.rearrange("(c p) f -> p c f", p=128)[:, :, sl]),
                        "dG0")
                    ctx.dinc(gp.dma_start(
                        out=wub[0][:, :].rearrange("p (c f) -> p c f", f=256),
                        in_=wup.rearrange("(c p) f -> p c f", p=128)[:, :, sl]),
                        "dG0")
            block.gpsimd(gpW)

            def t3(t, th=th):
                if th == 0:
                    ctx.wait(t, "dH2T")
                for grp in range(NG):
                    n = th * NG + grp
                    t.wait_ge(ctx.sem("dG0"), 32 * (n + 1))
                    for half in range(2):
                        ic = n * 2 + half
                        if ic >= 2:
                            t.wait_ge(ctx.sem("evGU"), ic - 1)
                        for kc in range(16):
                            ig = t.matmul(psG[ic % 2][:, :],
                                          wgb[0][:, 256 * kc + 128 * half:
                                                 256 * kc + 128 * (half + 1)],
                                          h2T[:, TOWN * kc + 512 * th:
                                              TOWN * kc + 512 * (th + 1)],
                                          start=(kc == 0), stop=(kc == 15))
                        ctx.inc(ig, "mGU")
                        for kc in range(16):
                            iu = t.matmul(psU[ic % 2][:, :],
                                          wub[0][:, 256 * kc + 128 * half:
                                                 256 * kc + 128 * (half + 1)],
                                          h2T[:, TOWN * kc + 512 * th:
                                              TOWN * kc + 512 * (th + 1)],
                                          start=(kc == 0), stop=(kc == 15))
                        ctx.inc(iu, "mGU")
            block.tensor(t3)

            def av1(a, th=th):
                a = ctx.ch(a, "cA")
                for icl in range(NIC):
                    n = th * NIC + icl
                    ctx.wait(a, "mGU", 2 * (n + 1))
                    if n >= 1:
                        ctx.wait(a, "evGU", n)
                    a.activation(silub[:, :], psG[n % 2][:, :], AF.Silu)
                    a.mark("aSilu")
            block.scalar(av1)

            def vv1(v, th=th):
                v = ctx.ch(v, "cV")
                for icl in range(NIC):
                    n = th * NIC + icl
                    ctx.wait(v, "aSilu", n + 1)
                    v.tensor_mul(interF[:, 512 * icl:512 * (icl + 1)],
                                 silub[:, :], psU[n % 2][:, :])
                    v.mark("evGU")
            block.vector(vv1)

            def gpW2(gp, th=th):
                for ft in range(4):
                    for icg in range(5):
                        n = (th * 4 + ft) * 5 + icg
                        if n >= 2:
                            ctx.wait(gp, "mDn", n - 1)
                        pr = n % 2
                        ctx.dinc(gp.dma_start(
                            out=wdb[pr][:, :].rearrange("p (c f) -> p c f", f=512),
                            in_=wdown.rearrange("(c p) f -> p c f", p=128)
                                [:, 8 * icg:8 * (icg + 1),
                                 512 * ft:512 * (ft + 1)]), f"dD{pr}")
            block.gpsimd(gpW2)

            def t4(t, th=th):
                ctx.wait(t, "evGU", (th + 1) * NIC)
                for ft in range(4):
                    k = th * 4 + ft
                    for icg in range(5):
                        n = k * 5 + icg
                        pr = n % 2
                        t.wait_ge(ctx.sem(f"dD{pr}"), 16 * (n // 2 + 1))
                        if icg == 0 and k >= 1:
                            t.wait_ge(ctx.sem("evY"), k)
                        for tc in range(4):
                            for i8 in range(8):
                                icl = icg * 8 + i8
                                iD = t.matmul(
                                    psY[tc][:, :],
                                    interF[:, 512 * icl + 128 * tc:
                                           512 * icl + 128 * (tc + 1)],
                                    wdb[pr][:, 512 * i8:512 * (i8 + 1)],
                                    start=(icl == 0), stop=(icl == NIC - 1))
                        ctx.inc(iD, "mDn")
            block.tensor(t4)

            def vv2(v, th=th):
                v = ctx.ch(v, "cV")
                if th == 0:
                    ctx.wait(v, "vHX")
                for ft in range(4):
                    k = th * 4 + ft
                    ctx.wait(v, "mDn", 5 * (k + 1))
                    for tc in range(4):
                        tg = th * 4 + tc
                        v.tensor_add(
                            hmid[:, D * tg + 512 * ft:D * tg + 512 * (ft + 1)],
                            hmid[:, D * tg + 512 * ft:D * tg + 512 * (ft + 1)],
                            psY[tc][:, :])
                    v.mark("evY")
            block.vector(vv2)

        def a4(a):
            a = ctx.ch(a, "cA")
            ctx.wait(a, "evY")
            for tc in range(8):
                a.activation(scr2[:, :], hmid[:, D * tc:D * (tc + 1)], AF.Square,
                             accum_out=sumt[:, tc:tc + 1])
                a.activation(scr2[:, :], hmid[:, D * tc:D * (tc + 1)], AF.Copy,
                             accum_out=invt[:, tc:tc + 1])
            a.mark("aSt")
        block.scalar(a4)

        def v5(v):
            v = ctx.ch(v, "cV")
            ctx.wait(v, "aSt")
            v.tensor_scalar_mul(invt[:, :], invt[:, :], 1.0 / D)
            v.tensor_mul(msq[:, :], invt[:, :], invt[:, :])
            v.tensor_scalar_mul(sumt[:, :], sumt[:, :], 1.0 / D)
            v.tensor_sub(sumt[:, :], sumt[:, :], msq[:, :])
            v.tensor_scalar_add(sumt[:, :], sumt[:, :], 1e-5)
            v.mark("vVar")
        block.vector(v5)

        def a5(a):
            a = ctx.ch(a, "cA")
            ctx.wait(a, "vVar")
            a.activation(sumt[:, :], sumt[:, :], AF.Ln)
            a.activation(sumt[:, :], sumt[:, :], AF.Exp, scale=-0.5)
            a.mark("aVar")
        block.scalar(a5)

        def v6(v):
            v = ctx.ch(v, "cV")
            ctx.wait(v, "aVar")
            v.tensor_mul(invt[:, :], invt[:, :], sumt[:, :])
            for tc in range(8):
                v.tensor_scalar(hmid[:, D * tc:D * (tc + 1)],
                                hmid[:, D * tc:D * (tc + 1)],
                                sumt[:, tc:tc + 1], invt[:, tc:tc + 1],
                                ALU.mult, ALU.subtract)
                v.mark("vX1")
        block.vector(v6)

        def sy4(sync):
            for tc in range(8):
                ctx.wait(sync, "vX1", tc + 1)
                ctx.dinc(sync.dma_start(out=x1o[128 * tc:128 * (tc + 1), :],
                                        in_=hmid[:, D * tc:D * (tc + 1)]), "dX1")
            ctx.wait(sync, "dX1")
        block.sync(sy4)

        p3.close()
    return nc


def launchD_inputs(inputs, cres):
    x = inputs["L0_attn_hidden_states"]
    ln2 = np.ascontiguousarray(inputs["L0_attn_ln2_w"].reshape(1, D))
    wout = np.ascontiguousarray(inputs["L0_attn_W_out"])
    wgate = np.ascontiguousarray(inputs["L0_attn_W_gate"])
    wup = np.ascontiguousarray(inputs["L0_attn_W_up"])
    wdown = np.ascontiguousarray(inputs["L0_attn_W_down"])
    gfull = np.zeros((4, 2048, 4096), BF)
    for cu in range(NCORES):
        gr = cres[cu]["gated"].reshape(NCH, 128, CU, 128)
        for ui in range(CU):
            u = 16 * cu + ui
            b, hv = u // 32, u % 32
            gfull[b, :, 128 * hv:128 * (hv + 1)] = gr[:, :, ui, :].reshape(2048, 128)
    maps = []
    for c in range(NCORES):
        b, hf = c // 2, c % 2
        sl = slice(1024 * hf, 1024 * (hf + 1))
        gT = np.ascontiguousarray(gfull[b, sl].T)
        maps.append(dict(gatedT=gT, wout=wout,
                         xown=np.ascontiguousarray(x[b, sl]), ln2=ln2,
                         wgate=wgate, wup=wup, wdown=wdown))
    return maps



def _emit_nvfp4_decode(ctx, v, a, wint, scal, wdec, iscr, bscr):
    lo, hi = iscr
    wb, m8, r4 = bscr
    v.tensor_scalar(lo[:, :], wint[:, :], 15, None, ALU.bitwise_and)
    v.tensor_scalar(hi[:, :], wint[:, :], 4, 15, ALU.logical_shift_right,
                    ALU.bitwise_and)
    wbv = wb[:, :].rearrange("p (a two) -> p a two", two=2)
    v.tensor_copy(wbv[:, :, 0], lo[:, :])
    v.tensor_copy(wbv[:, :, 1], hi[:, :])
    v.tensor_scalar(m8[:, :], wb[:, :], 8.0, None, ALU.is_ge)
    v.scalar_tensor_tensor(wdec[:, :], m8[:, :], -8.0, wb[:, :],
                           ALU.mult, ALU.add)
    v.tensor_scalar(r4[:, :], wdec[:, :], 4.0, 0.0, ALU.subtract, ALU.max)
    v.tensor_scalar(wb[:, :], wdec[:, :], 6.0, 0.0, ALU.subtract, ALU.max)
    v.scalar_tensor_tensor(wdec[:, :], wdec[:, :], 0.5, wb[:, :],
                           ALU.mult, ALU.add)
    v.scalar_tensor_tensor(wdec[:, :], r4[:, :], 0.5, wdec[:, :],
                           ALU.mult, ALU.add)
    v.tensor_scalar(m8[:, :], m8[:, :], -2.0, 1.0, ALU.mult, ALU.add)
    v.tensor_mul(wdec[:, :], wdec[:, :], m8[:, :])
    v.tensor_mul(wdec[:, :], wdec[:, :],
                 scal[:, :].unsqueeze(2).broadcast_to([128, 64, 32]))


def build_launchE1():
    nc = bass.Bass("TRN2", target_bir_lowering=False)
    x1f = nc.dram_tensor("x1f", [8192, D], F32, kind="ExternalInput").ap()
    gwp = nc.dram_tensor("gwp", [2048, 1024], I32, kind="ExternalInput").ap()
    gsc = nc.dram_tensor("gsc", [2048, 64], F32, kind="ExternalInput").ap()
    uwp = nc.dram_tensor("uwp", [2048, 1024], I32, kind="ExternalInput").ap()
    usc = nc.dram_tensor("usc", [2048, 64], F32, kind="ExternalInput").ap()
    dwp = nc.dram_tensor("dwp", [2048, 1024], I32, kind="ExternalInput").ap()
    dsc = nc.dram_tensor("dsc", [2048, 64], F32, kind="ExternalInput").ap()
    part = nc.dram_tensor("part", [D, 8192], F32, kind="ExternalOutput").ap()
    x1bf = nc.dram_tensor("x1bf", [8192, D], BF16).ap()
    wgtT = nc.dram_tensor("wgtT", [D, 2048], BF16).ap()
    wutT = nc.dram_tensor("wutT", [D, 2048], BF16).ap()
    wdtT = nc.dram_tensor("wdtT", [2048, D], BF16).ap()

    es = contextlib.ExitStack()
    with es:
        ctx = Ctx(nc, es)
        sb = lambda n, s, d: es.enter_context(nc.sbuf_tensor(n, s, d))

        wint = [sb(f"wint{i}", [128, 1024], I32) for i in range(2)]
        scal = [sb(f"scal{i}", [128, 64], F32) for i in range(2)]
        lo = sb("lo", [128, 1024], I32)
        hi = sb("hi", [128, 1024], I32)
        wb_ = sb("wb_", [128, 2048], BF16)
        m8 = sb("m8", [128, 2048], BF16)
        r4 = sb("r4", [128, 2048], BF16)
        wdec = sb("wdec", [128, 2048], BF16)
        wTs = [sb(f"wTs{i}", [128, 16 * 128], BF16) for i in range(2)]
        identb2 = sb("identb2", [128, 128], BF16)
        x1T = sb("x1T", [128, 16 * 1024], BF16)
        interE = sb("interE", [128, 16 * 1024], BF16)
        gelb = sb("gelb", [128, 512], BF16)
        wstr = [sb(f"wstr{i}", [128, 16 * 128], BF16) for i in range(2)]
        oute = [sb(f"oute{i}", [128, 512], F32) for i in range(2)]

        psT1 = es.enter_context(nc.psum_tensor("psT1", [128, 1024], BF16))
        psT2 = es.enter_context(nc.psum_tensor("psT2", [128, 1024], BF16))
        psE = [es.enter_context(nc.psum_tensor(f"psE{i}", [128, 512], F32))
               for i in range(4)]

        block = es.enter_context(nc.Block())

        def gp0(gp):
            g = ctx.ch(gp, "cG")
            make_identity(g, identb2[:, :])
            g.mark("gI")
            ctx.dinc(gp.dma_start(out=x1bf[:, :], in_=x1f[:, :]), "dXc")
        block.gpsimd(gp0)

        WSPECS = [(gwp, gsc, wgtT), (uwp, usc, wutT), (dwp, dsc, wdtT)]

        def syDec(sync):
            for wi in range(48):
                wsel, ssel, _ = WSPECS[wi // 16]
                ch = wi % 16
                if wi >= 2:
                    ctx.wait(sync, "vDec", wi - 1)
                ctx.dinc(sync.dma_start(out=wint[wi % 2][:, :],
                                        in_=wsel[128 * ch:128 * (ch + 1), :]),
                         f"dP{wi % 2}")
                ctx.dinc(sync.dma_start(out=scal[wi % 2][:, :],
                                        in_=ssel[128 * ch:128 * (ch + 1), :]),
                         f"dP{wi % 2}")
        block.sync(syDec)

        for wi in range(48):
            def vdec(v, wi=wi):
                v = ctx.ch(v, "cV")
                a = None
                v.wait_ge(ctx.sem(f"dP{wi % 2}"), 32 * (wi // 2 + 1))
                if wi >= 1:
                    ctx.wait(v, "tDec", 16 * wi)
                _emit_nvfp4_decode(ctx, v, a, wint[wi % 2], scal[wi % 2], wdec,
                                   (lo, hi), (wb_, m8, r4))
                v.mark("vDec")
            block.vector(vdec)

            def tdec(t, wi=wi):
                ctx.wait(t, "vDec", wi + 1)
                ctx.wait(t, "gI")
                if wi >= 1:
                    ctx.wait(t, "aDec", 16 * wi)
                for j in range(16):
                    pdst = psT1 if j % 2 == 0 else psT2
                    if j >= 2:
                        ctx.wait(t, "aDec", 16 * wi + j - 1)
                    ctx.inc(t.transpose(pdst[:, 0:128],
                                        wdec[:, 128 * j:128 * (j + 1)],
                                        identb2[:, :]), "tDec")
            block.tensor(tdec)

            def adec(a, wi=wi):
                a = ctx.ch(a, "cA")
                if wi >= 2:
                    a.wait_ge(ctx.sem(f"dWT{wi % 2}"), 16 * ((wi - 2) // 2 + 1))
                for j in range(16):
                    ctx.wait(a, "tDec", 16 * wi + j + 1)
                    pdst = psT1 if j % 2 == 0 else psT2
                    a.activation(wTs[wi % 2][:, 128 * j:128 * (j + 1)],
                                 pdst[:, 0:128], AF.Copy)
                    a.mark("aDec")
            block.scalar(adec)

            def gpdec(gp, wi=wi):
                ctx.wait(gp, "aDec", 16 * (wi + 1))
                _, _, wdst = WSPECS[wi // 16]
                ch = wi % 16
                if wi // 16 < 2:
                    dst = wdst.rearrange("(c p) n -> p c n", p=128)[:, :,
                                                                   128 * ch:128 * (ch + 1)]
                else:
                    dst = wdst.rearrange("(c p) n -> p c n", p=128)[:, :,
                                                                   128 * ch:128 * (ch + 1)]
                ctx.dinc(gp.dma_start(
                    out=dst,
                    in_=wTs[wi % 2][:, :].rearrange("p (c n) -> p c n", n=128)),
                    f"dWT{wi % 2}")
            block.gpsimd(gpdec)

        def chunk_done(n):
            te, j = n // 48, n % 48
            if j < 32:
                nc_ = j // 2
                if j % 2 == 0:
                    return ("mgu", 64 * te + 4 * nc_ + 3)
                return ("mgu", 64 * te + 4 * nc_ + 4)
            return ("mdn", 32 * te + 2 * (j - 32) + 2)

        for s_ in ("mgu", "mdn", "egu", "edn", "agel", "einter"):
            ctx.sem(s_)

        def syX(sync):
            ctx.wait(sync, "dXc")
            for te in range(8):
                if te >= 1:
                    ctx.wait(sync, "mdn", 32 * te)
                for j in range(16):
                    ctx.dinc(sync.dma_start(
                        out=x1T[:, 1024 * j:1024 * (j + 1)],
                        in_=x1bf[1024 * te:1024 * (te + 1), 128 * j:128 * (j + 1)],
                        transpose=True), f"dXT{te % 2}")
        block.sync(syX)

        def gpw(gp):
            ctx.wait(gp, "dWT0")
            ctx.wait(gp, "dWT1")
            for n in range(8 * 48):
                te, j = n // 48, n % 48
                if n >= 2:
                    snm, sval = chunk_done(n - 2)
                    ctx.wait(gp, snm, sval)
                if j < 32:
                    wsrc = wgtT if j % 2 == 0 else wutT
                    nc_ = j // 2
                else:
                    wsrc = wdtT
                    nc_ = j - 32
                ctx.dinc(gp.dma_start(
                    out=wstr[n % 2][:, :].rearrange("p (c n) -> p c n", n=128),
                    in_=wsrc.rearrange("(c p) n -> p c n", p=128)
                        [:, :, 128 * nc_:128 * (nc_ + 1)]), f"dWS{n % 2}")
        block.gpsimd(gpw)

        def tmm(t):
            for te in range(8):
                t.wait_ge(ctx.sem(f"dXT{te % 2}"), 256 * (te // 2 + 1))
                if te >= 1:
                    ctx.wait(t, "edn", 32 * te)
                for nc_ in range(16):
                    for tt in range(2):
                        k = 32 * te + 2 * nc_ + tt
                        if k >= 2:
                            ctx.wait(t, "egu", k - 1)
                        ng = te * 48 + 2 * nc_
                        nu = te * 48 + 2 * nc_ + 1
                        t.wait_ge(ctx.sem(f"dWS{ng % 2}"), 16 * (ng // 2 + 1))
                        t.wait_ge(ctx.sem(f"dWS{nu % 2}"), 16 * (nu // 2 + 1))
                        for kc in range(16):
                            ig = t.matmul(
                                psE[2 * (k % 2)][:, :],
                                wstr[ng % 2][:, 128 * kc:128 * (kc + 1)],
                                x1T[:, 1024 * kc + 512 * tt:
                                    1024 * kc + 512 * (tt + 1)],
                                start=(kc == 0), stop=(kc == 15))
                        ctx.inc(ig, "mgu")
                        for kc in range(16):
                            iu = t.matmul(
                                psE[2 * (k % 2) + 1][:, :],
                                wstr[nu % 2][:, 128 * kc:128 * (kc + 1)],
                                x1T[:, 1024 * kc + 512 * tt:
                                    1024 * kc + 512 * (tt + 1)],
                                start=(kc == 0), stop=(kc == 15))
                        ctx.inc(iu, "mgu")
                ctx.wait(t, "einter", te + 1)
                for nc_ in range(16):
                    nd = te * 48 + 32 + nc_
                    t.wait_ge(ctx.sem(f"dWS{nd % 2}"), 16 * (nd // 2 + 1))
                    for tt in range(2):
                        d = 32 * te + 2 * nc_ + tt
                        if d >= 2:
                            ctx.wait(t, "edn", d - 1)
                        for ic in range(16):
                            iD = t.matmul(
                                psE[2 * (d % 2)][:, :],
                                wstr[nd % 2][:, 128 * ic:128 * (ic + 1)],
                                interE[:, 1024 * ic + 512 * tt:
                                       1024 * ic + 512 * (tt + 1)],
                                start=(ic == 0), stop=(ic == 15))
                        ctx.inc(iD, "mdn")
        block.tensor(tmm)

        def agelout(a):
            a = ctx.ch(a, "cA")
            for te in range(8):
                for nc_ in range(16):
                    for tt in range(2):
                        k = 32 * te + 2 * nc_ + tt
                        ctx.wait(a, "mgu", 2 * k + 1)
                        if k >= 1:
                            ctx.wait(a, "egu", k)
                        a.activation(gelb[:, :], psE[2 * (k % 2)][:, :],
                                     AF.Gelu_apprx_tanh)
                        a.mark("agel")
                for nc_ in range(16):
                    for tt in range(2):
                        d = 32 * te + 2 * nc_ + tt
                        ctx.wait(a, "mdn", d + 1)
                        if d >= 2:
                            a.wait_ge(ctx.sem(f"dOut{d % 2}"),
                                      16 * ((d - 2) // 2 + 1))
                        a.activation(oute[d % 2][:, :], psE[2 * (d % 2)][:, :],
                                     AF.Copy)
                        a.mark("edn")
                        ctx.dinc(a.dma_start(
                            out=part[128 * nc_:128 * (nc_ + 1),
                                     1024 * te + 512 * tt:
                                     1024 * te + 512 * (tt + 1)],
                            in_=oute[d % 2][:, :]), f"dOut{d % 2}")
        block.scalar(agelout)

        def vint(v):
            v = ctx.ch(v, "cV")
            for te in range(8):
                if te >= 1:
                    ctx.wait(v, "mdn", 32 * te)
                for nc_ in range(16):
                    for tt in range(2):
                        k = 32 * te + 2 * nc_ + tt
                        ctx.wait(v, "agel", k + 1)
                        ctx.wait(v, "mgu", 2 * k + 2)
                        v.tensor_mul(interE[:, 1024 * nc_ + 512 * tt:
                                            1024 * nc_ + 512 * (tt + 1)],
                                     gelb[:, :], psE[2 * (k % 2) + 1][:, :])
                        v.mark("egu")
                v.mark("einter")
        block.vector(vint)

        def syF(sync):
            ctx.wait(sync, "dOut0")
            ctx.wait(sync, "dOut1")
        block.sync(syF)
    return nc



def _emit_postln_store(ctx, block, hbuf, scrb_, sumt, invt, msq, outdram, D_=D,
                       ntile=8, marks=("aStP", "vVarP", "aVarP", "vXP", "dXP"),
                       prewait=None):
    aSt, vVar, aVar, vX, dX = marks

    def a4(a):
        a = ctx.ch(a, "cA")
        if prewait:
            ctx.wait(a, prewait)
        for tc in range(ntile):
            a.activation(scrb_[:, :], hbuf[:, D_ * tc:D_ * (tc + 1)], AF.Square,
                         accum_out=sumt[:, tc:tc + 1])
            a.activation(scrb_[:, :], hbuf[:, D_ * tc:D_ * (tc + 1)], AF.Copy,
                         accum_out=invt[:, tc:tc + 1])
        a.mark(aSt)
    block.scalar(a4)

    def v5(v):
        v = ctx.ch(v, "cV")
        ctx.wait(v, aSt)
        v.tensor_scalar_mul(invt[:, :], invt[:, :], 1.0 / D_)
        v.tensor_mul(msq[:, :], invt[:, :], invt[:, :])
        v.tensor_scalar_mul(sumt[:, :], sumt[:, :], 1.0 / D_)
        v.tensor_sub(sumt[:, :], sumt[:, :], msq[:, :])
        v.tensor_scalar_add(sumt[:, :], sumt[:, :], 1e-5)
        v.mark(vVar)
    block.vector(v5)

    def a5(a):
        a = ctx.ch(a, "cA")
        ctx.wait(a, vVar)
        a.activation(sumt[:, :], sumt[:, :], AF.Ln)
        a.activation(sumt[:, :], sumt[:, :], AF.Exp, scale=-0.5)
        a.mark(aVar)
    block.scalar(a5)

    def v6(v):
        v = ctx.ch(v, "cV")
        ctx.wait(v, aVar)
        v.tensor_mul(invt[:, :], invt[:, :], sumt[:, :])
        for tc in range(ntile):
            v.tensor_scalar(hbuf[:, D_ * tc:D_ * (tc + 1)],
                            hbuf[:, D_ * tc:D_ * (tc + 1)],
                            sumt[:, tc:tc + 1], invt[:, tc:tc + 1],
                            ALU.mult, ALU.subtract)
            v.mark(vX)
    block.vector(v6)

    def sy4(sync):
        for tc in range(ntile):
            ctx.wait(sync, vX, tc + 1)
            ctx.dinc(sync.dma_start(out=outdram[128 * tc:128 * (tc + 1), :],
                                    in_=hbuf[:, D_ * tc:D_ * (tc + 1)]), dX)
        ctx.wait(sync, dX)
    block.sync(sy4)


def build_launchE2():
    nc = bass.Bass("TRN2", target_bir_lowering=False)
    parts = nc.dram_tensor("parts", [8, D, TOWN], F32, kind="ExternalInput").ap()
    x1own = nc.dram_tensor("x1own", [TOWN, D], F32, kind="ExternalInput").ap()
    x2o = nc.dram_tensor("x2o", [TOWN, D], F32, kind="ExternalOutput").ap()

    es = contextlib.ExitStack()
    with es:
        ctx = Ctx(nc, es)
        sb = lambda n, s, d: es.enter_context(nc.sbuf_tensor(n, s, d))
        ffacc = sb("ffacc", [128, 16 * TOWN], F32)
        htm = sb("htm", [128, 8 * D], F32)
        scrb_ = sb("scrE", [128, D], F32)
        sumt = sb("sumtE", [128, 8], F32)
        invt = sb("invtE", [128, 8], F32)
        msq = sb("msqE", [128, 8], F32)
        identf3 = sb("identf3", [128, 128], F32)
        psF = [es.enter_context(nc.psum_tensor(f"psF{i}", [128, 512], F32))
               for i in range(2)]

        block = es.enter_context(nc.Block())

        def gp0(gp):
            g = ctx.ch(gp, "cG")
            make_identity(g, identf3[:, :])
            g.mark("gI2")
            for p in range(8):
                if p >= 1:
                    ctx.wait(gp, "dAcc")
                ctx.dinc(gp.dma_start(
                    out=ffacc[:, :].rearrange("p (c t) -> p c t", t=TOWN),
                    in_=parts[p].rearrange("(c p) t -> p c t", p=128),
                    accum_op=(ALU.add if p > 0 else ALU.bypass)), "dAcc")
        block.gpsimd(gp0)

        def t0(t):
            ctx.wait(t, "dAcc")
            ctx.wait(t, "gI2")
            for fc in range(16):
                for tc in range(8):
                    n = fc * 8 + tc
                    if n >= 2:
                        ctx.wait(t, "aTr", n - 1)
                    ctx.inc(t.transpose(psF[n % 2][:, 0:128],
                                        ffacc[:, TOWN * fc + 128 * tc:
                                              TOWN * fc + 128 * (tc + 1)],
                                        identf3[:, :]), "tTr")
        block.tensor(t0)

        def a0(a):
            a = ctx.ch(a, "cA")
            for fc in range(16):
                for tc in range(8):
                    n = fc * 8 + tc
                    ctx.wait(a, "tTr", n + 1)
                    a.activation(htm[:, D * tc + 128 * fc:D * tc + 128 * (fc + 1)],
                                 psF[n % 2][:, 0:128], AF.Copy)
                    a.mark("aTr")
        block.scalar(a0)

        def sy1(sync):
            ctx.wait(sync, "tTr")
            for tc in range(8):
                ctx.dinc(sync.dma_start(out=ffacc[:, D * tc:D * (tc + 1)],
                                        in_=x1own[128 * tc:128 * (tc + 1), :]),
                         "dX1r")
        block.sync(sy1)

        def v0(v):
            v = ctx.ch(v, "cV")
            ctx.wait(v, "aTr")
            ctx.wait(v, "dX1r")
            for tc in range(8):
                v.tensor_add(htm[:, D * tc:D * (tc + 1)],
                             htm[:, D * tc:D * (tc + 1)],
                             ffacc[:, D * tc:D * (tc + 1)])
            v.mark("vRes")
        block.vector(v0)

        _emit_postln_store(ctx, block, htm, scrb_, sumt, invt, msq, x2o,
                           prewait="vRes")
    return nc



def _emit_fm_matmul(ctx, block, tag, wdram, x_sb, KCH, FCH, evac,
                    psums=None, extra_twait=None, prev_tag=None):
    ps = psums

    def gpw(gp, tag=tag):
        for fc in range(FCH):
            if fc >= 2:
                ctx.wait(gp, f"mm_{tag}", 2 * (fc - 1))
            elif prev_tag is not None:
                ctx.wait(gp, f"mm_{prev_tag}", ctx.cnt[f"mm_{prev_tag}"])
            ctx.dinc(gp.dma_start(
                out=ctx.bufs["wst"][fc % 2][:, 0:KCH * 128]
                    .rearrange("p (c f) -> p c f", f=128),
                in_=wdram.rearrange("(c p) f -> p c f", p=128)
                    [:, :, 128 * fc:128 * (fc + 1)]), f"dw_{tag}{fc % 2}")
    block.gpsimd(gpw)

    def tmm(t, tag=tag):
        if extra_twait:
            extra_twait(t)
        if prev_tag is not None:
            ctx.wait(t, f"ev_{prev_tag}", ctx.cnt[f"ev_{prev_tag}"])
        for fc in range(FCH):
            t.wait_ge(ctx.sem(f"dw_{tag}{fc % 2}"), 16 * (fc // 2 + 1))
            for tt in range(2):
                n = fc * 2 + tt
                if n >= 4:
                    ctx.wait(t, f"ev_{tag}", n - 3)
                pd = ps[n % 4]
                for kc in range(KCH):
                    inst = t.matmul(
                        pd[:, :],
                        ctx.bufs["wst"][fc % 2][:, 128 * kc:128 * (kc + 1)],
                        x_sb[:, 1024 * kc + 512 * tt:1024 * kc + 512 * (tt + 1)],
                        start=(kc == 0), stop=(kc == KCH - 1))
                ctx.inc(inst, f"mm_{tag}")
    block.tensor(tmm)

    def aev(a, tag=tag):
        a = ctx.ch(a, "cA")
        for fc in range(FCH):
            for tt in range(2):
                n = fc * 2 + tt
                ctx.wait(a, f"mm_{tag}", n + 1)
                evac(a, fc, tt, ps[n % 4])
                a.mark(f"ev_{tag}")
    block.scalar(aev)


def build_launchF1():
    nc = bass.Bass("TRN2", target_bir_lowering=False)
    x2own = nc.dram_tensor("x2own", [TOWN, D], F32, kind="ExternalInput").ap()
    qaT = nc.dram_tensor("qaT", [D, 1536], F32, kind="ExternalInput").ap()
    lnqa = nc.dram_tensor("lnqa", [128, 12], F32, kind="ExternalInput").ap()
    qbTp = nc.dram_tensor("qbTp", [1536, 4096], F32, kind="ExternalInput").ap()
    kvaTp = nc.dram_tensor("kvaTp", [D, 2560], F32, kind="ExternalInput").ap()
    lnkv = nc.dram_tensor("lnkv", [128, 4], F32, kind="ExternalInput").ap()
    kvbTp = nc.dram_tensor("kvbTp", [512, 4096], F32, kind="ExternalInput").ap()
    spqT = nc.dram_tensor("spqT", [D, D], F32, kind="ExternalInput").ap()
    spkT = nc.dram_tensor("spkT", [D, D], F32, kind="ExternalInput").ap()
    spvT = nc.dram_tensor("spvT", [D, D], F32, kind="ExternalInput").ap()
    gw0 = nc.dram_tensor("gw0", [128, 16], F32, kind="ExternalInput").ap()
    gw1 = nc.dram_tensor("gw1", [128, 16], F32, kind="ExternalInput").ap()
    gb = nc.dram_tensor("gb", [1, 2], F32, kind="ExternalInput").ap()
    posr = nc.dram_tensor("posr", [1, TOWN], F32, kind="ExternalInput").ap()
    invf = nc.dram_tensor("invf", [32, 1], F32, kind="ExternalInput").ap()

    qn_o = nc.dram_tensor("qn_o", [2048, TOWN], BF16, kind="ExternalOutput").ap()
    qp_o = nc.dram_tensor("qp_o", [2048, TOWN], BF16, kind="ExternalOutput").ap()
    kn_o = nc.dram_tensor("kn_o", [2048, TOWN], BF16, kind="ExternalOutput").ap()
    kp_o = nc.dram_tensor("kp_o", [2048, TOWN], BF16, kind="ExternalOutput").ap()
    v1_o = nc.dram_tensor("v1_o", [2048, TOWN], BF16, kind="ExternalOutput").ap()
    pq_o = nc.dram_tensor("pq_o", [2048, TOWN], BF16, kind="ExternalOutput").ap()
    pk_o = nc.dram_tensor("pk_o", [2048, TOWN], BF16, kind="ExternalOutput").ap()
    pv_o = nc.dram_tensor("pv_o", [2048, TOWN], BF16, kind="ExternalOutput").ap()
    g0_o = nc.dram_tensor("g0_o", [1, TOWN], F32, kind="ExternalOutput").ap()
    g1_o = nc.dram_tensor("g1_o", [1, TOWN], F32, kind="ExternalOutput").ap()
    x2bf = nc.dram_tensor("x2bf", [TOWN, D], BF16).ap()

    es = contextlib.ExitStack()
    with es:
        ctx = Ctx(nc, es)
        ctx.bufs = {}
        sb = lambda n, s, d: es.enter_context(nc.sbuf_tensor(n, s, d))

        x2T = sb("x2T", [128, 16 * TOWN], BF16)
        qmid = sb("qmid", [128, 12 * TOWN], BF16)
        kvn = sb("kvn", [128, 4 * TOWN], BF16)
        qpe = sb("qpe", [128, 16 * TOWN], BF16)
        kpe = sb("kpe", [128, 16 * TOWN], BF16)
        otile = [sb(f"otile{i}", [128, 512], BF16) for i in range(2)]
        scrF = sb("scrF", [128, TOWN], BF16)
        rowF = sb("rowF", [1, TOWN], F32)
        invbcF = sb("invbcF", [128, TOWN], F32)
        lnqac = sb("lnqac", [128, 12], F32)
        lnkvc = sb("lnkvc", [128, 4], F32)
        onesbF = sb("onesbF", [128, 1], BF16)
        onesfF = sb("onesfF", [1, 128], F32)
        posrow = sb("posrow", [1, TOWN], F32)
        invfc = sb("invfc", [32, 1], F32)
        frq = sb("frq", [32, TOWN], F32)
        frn = sb("frn", [32, TOWN], F32)
        sin32 = sb("sin32", [32, TOWN], BF16)
        cos32 = sb("cos32", [32, TOWN], BF16)
        sc32f = sb("sc32f", [32, TOWN], F32)
        REPc = sb("REPc", [32, 128], BF16)
        cosT = sb("cosT", [128, TOWN], BF16)
        sinT = sb("sinT", [128, TOWN], BF16)
        rot = sb("rot", [128, TOWN], BF16)
        gbt = sb("gbt", [1, 2], F32)
        bdif = sb("bdif", [1, 1], F32)
        g0t = sb("g0t", [1, TOWN], F32)
        g1t = sb("g1t", [1, TOWN], F32)
        ctx.bufs["wst"] = [sb(f"wstF{i}", [128, 16 * 128], BF16)
                           for i in range(2)]
        wstg = [sb(f"wstg{i}", [128, 16], BF16) for i in range(2)]

        psA4 = [es.enter_context(nc.psum_tensor(f"psA4{i}", [128, 512], F32))
                for i in range(4)]
        psR1 = es.enter_context(nc.psum_tensor("psR1", [1, 512], F32))
        psR2 = es.enter_context(nc.psum_tensor("psR2", [1, 512], F32))
        psBc = es.enter_context(nc.psum_tensor("psBc", [128, 512], F32))
        psRo = es.enter_context(nc.psum_tensor("psRo", [128, 512], F32))

        block = es.enter_context(nc.Block())

        def sy0(sync):
            ctx.dinc(sync.dma_start(out=lnqac[:, :], in_=lnqa[:, :]), "dS")
            ctx.dinc(sync.dma_start(out=lnkvc[:, :], in_=lnkv[:, :]), "dS")
            ctx.dinc(sync.dma_start(out=posrow[:, :], in_=posr[:, :]), "dS")
            ctx.dinc(sync.dma_start(out=invfc[:, :], in_=invf[:, :]), "dS")
            ctx.dinc(sync.dma_start(out=gbt[:, :], in_=gb[:, :]), "dS")
        block.sync(sy0)

        def gp0(gp):
            g = ctx.ch(gp, "cG")
            g.memset(onesbF[:, :], 1.0)
            g.memset(onesfF[:, :], 1.0)
            g.memset(REPc[:, :], 0.0)
            for kk in range(2):
                g.affine_select(out=REPc[:, :], in_=REPc[:, :],
                                compare_op=ALU.not_equal, fill=1.0,
                                base=32 * kk, pattern=[[-1, 128]],
                                channel_multiplier=1)
            g.mark("gE")
            ctx.dinc(gp.dma_start(out=x2bf[:, :], in_=x2own[:, :]), "dXc")
            ctx.dinc(gp.dma_start(out=wstg[0][:, :], in_=gw0[:, :]), "dgw")
            ctx.dinc(gp.dma_start(out=wstg[1][:, :], in_=gw1[:, :]), "dgw")
        block.gpsimd(gp0)

        def syT(sync):
            ctx.wait(sync, "dXc")
            for j in range(16):
                ctx.dinc(sync.dma_start(out=x2T[:, TOWN * j:TOWN * (j + 1)],
                                        in_=x2bf[:, 128 * j:128 * (j + 1)],
                                        transpose=True), "dT")
        block.sync(syT)

        def t_rt(t):
            ctx.wait(t, "dS")
            ctx.wait(t, "gE")
            for tt in range(2):
                if tt == 1:
                    ctx.wait(t, "aRT1", 1)
                ctx.inc(t.matmul(psBc[0:32, :], onesfF[:, 0:32],
                                 posrow[:, 512 * tt:512 * (tt + 1)],
                                 start=True, stop=True), "tRT")
        block.tensor(t_rt)

        def a_rt(a):
            a = ctx.ch(a, "cA")
            for tt in range(2):
                ctx.wait(a, "tRT", tt + 1)
                a.activation(frq[0:32, 512 * tt:512 * (tt + 1)], psBc[0:32, :],
                             AF.Copy)
                a.mark("aRT1")
        block.scalar(a_rt)

        def v_rt(v):
            v = ctx.ch(v, "cV")
            ctx.wait(v, "aRT1")
            v.tensor_scalar_mul(frq[:, :], frq[:, :], invfc[:, 0:1])
            v.tensor_scalar_mul(frn[:, :], frq[:, :], float(1.0 / (2 * np.pi)))
            v.tensor_scalar(frn[:, :], frn[:, :], 12582912.0, 12582912.0,
                            ALU.add, ALU.subtract)
            v.scalar_tensor_tensor(frq[:, :], frn[:, :], -6.28125, frq[:, :],
                                   ALU.mult, ALU.add)
            v.scalar_tensor_tensor(frq[:, :], frn[:, :],
                                   float(-(2 * np.pi - 6.28125)), frq[:, :],
                                   ALU.mult, ALU.add)
            v.tensor_scalar(frq[:, :], frq[:, :], float(-np.pi), float(np.pi),
                            ALU.max, ALU.min)
            v.mark("vRT")
        block.vector(v_rt)

        def a_rt2(a):
            a = ctx.ch(a, "cA")
            ctx.wait(a, "vRT")
            a.activation(sin32[:, :], frq[:, :], AF.Sin)
            a.activation(sc32f[:, :], frq[:, :], AF.Sin, scale=0.5)
            a.activation(sc32f[:, :], sc32f[:, :], AF.Square)
            a.mark("aRT2")
        block.scalar(a_rt2)

        def v_rt2(v):
            v = ctx.ch(v, "cV")
            ctx.wait(v, "aRT2")
            v.tensor_scalar(cos32[:, :], sc32f[:, :], -2.0, 1.0, ALU.mult,
                            ALU.add)
            v.mark("vRT2")
        block.vector(v_rt2)

        def t_rt2(t):
            ctx.wait(t, "vRT2")
            for which in range(2):
                for tt in range(2):
                    n = 2 * which + tt
                    if n >= 1:
                        ctx.wait(t, "aRT3", n)
                    src = cos32 if which == 0 else sin32
                    ctx.inc(t.matmul(psRo[:, 0:512], REPc[:, :],
                                     src[:, 512 * tt:512 * (tt + 1)],
                                     start=True, stop=True), "tRT2")
        block.tensor(t_rt2)

        def a_rt3(a):
            a = ctx.ch(a, "cA")
            for which in range(2):
                for tt in range(2):
                    ctx.wait(a, "tRT2", 2 * which + tt + 1)
                    dst = cosT if which == 0 else sinT
                    a.activation(dst[:, 512 * tt:512 * (tt + 1)], psRo[:, 0:512],
                                 AF.Copy)
                    a.mark("aRT3")
        block.scalar(a_rt3)

        def ev_qa(a, fc, tt, pd):
            a.activation(qmid[:, TOWN * fc + 512 * tt:TOWN * fc + 512 * (tt + 1)],
                         pd[:, :], AF.Copy)
        _emit_fm_matmul(ctx, block, "qa", qaT, x2T, 16, 12, ev_qa, psums=psA4,
                        extra_twait=lambda t: ctx.wait(t, "dT"))

        def emit_fm_rmsnorm(tag, buf, NFC, lncol, dim, prev_mm, prevrow_mark):
            def a_1(a):
                a = ctx.ch(a, "cA")
                ctx.wait(a, f"ev_{tag}")
                for fc in range(NFC):
                    if fc >= 1:
                        ctx.wait(a, f"t1_{tag}", 2 * fc)
                    a.activation(scrF[:, :], buf[:, TOWN * fc:TOWN * (fc + 1)],
                                 AF.Square)
                    a.mark(f"a2_{tag}")
            block.scalar(a_1)

            def t_1(t):
                ctx.wait(t, f"mm_{tag}", ctx.cnt[f"mm_{tag}"])
                if prevrow_mark is not None:
                    ctx.wait(t, prevrow_mark, ctx.cnt[prevrow_mark])
                for fc in range(NFC):
                    ctx.wait(t, f"a2_{tag}", fc + 1)
                    for tt in range(2):
                        ctx.inc(t.matmul((psR1 if tt == 0 else psR2)[:, :],
                                         onesbF[:, :],
                                         scrF[:, 512 * tt:512 * (tt + 1)],
                                         start=(fc == 0), stop=(fc == NFC - 1)),
                                f"t1_{tag}")
            block.tensor(t_1)

            def a_2(a):
                a = ctx.ch(a, "cA")
                ctx.wait(a, f"t1_{tag}")
                a.activation(rowF[:, 0:512], psR1[:, :], AF.Copy)
                a.activation(rowF[:, 512:1024], psR2[:, :], AF.Copy)
                a.mark(f"arow_{tag}")
            block.scalar(a_2)

            def v_1(v):
                v = ctx.ch(v, "cV")
                ctx.wait(v, f"arow_{tag}")
                v.tensor_scalar(rowF[:, :], rowF[:, :], 1.0 / dim, 1e-6,
                                ALU.mult, ALU.add)
                v.mark(f"vrow_{tag}")
            block.vector(v_1)

            def a_3(a):
                a = ctx.ch(a, "cA")
                ctx.wait(a, f"vrow_{tag}")
                a.activation(rowF[:, :], rowF[:, :], AF.Ln)
                a.activation(rowF[:, :], rowF[:, :], AF.Exp, scale=-0.5)
                a.mark(f"ainv_{tag}")
            block.scalar(a_3)

            def t_2(t):
                ctx.wait(t, f"ainv_{tag}")
                for tt in range(2):
                    if tt == 1:
                        ctx.wait(t, f"abc_{tag}", ctx.cnt[f"abc_{tag}"] + 1
                                 if False else 1 + ctx.cnt.get("__", 0))
                    ctx.inc(t.matmul(psBc[:, :], onesfF[:, :],
                                     rowF[:, 512 * tt:512 * (tt + 1)],
                                     start=True, stop=True), f"tbc_{tag}")
            block.tensor(t_2)

            def a_4(a):
                a = ctx.ch(a, "cA")
                for tt in range(2):
                    ctx.wait(a, f"tbc_{tag}", tt + 1)
                    a.activation(invbcF[:, 512 * tt:512 * (tt + 1)], psBc[:, :],
                                 AF.Copy)
                    a.mark(f"abc_{tag}")
            block.scalar(a_4)

            def v_2(v):
                v = ctx.ch(v, "cV")
                ctx.wait(v, f"abc_{tag}")
                for fc in range(NFC):
                    v.scalar_tensor_tensor(buf[:, TOWN * fc:TOWN * (fc + 1)],
                                           buf[:, TOWN * fc:TOWN * (fc + 1)],
                                           lncol[:, fc:fc + 1], invbcF[:, :],
                                           ALU.mult, ALU.mult)
                v.mark(f"vn_{tag}")
            block.vector(v_2)

        emit_fm_rmsnorm("qa", qmid, 12, lnqac, 1536.0, None, None)

        def ev_qb(a, fc, tt, pd):
            n = fc * 2 + tt
            if fc < 16:
                if n >= 2:
                    a.wait_ge(ctx.sem(f"dqn{n % 2}"), 16 * ((n - 2) // 2 + 1))
                a.activation(otile[n % 2][:, :], pd[:, :], AF.Copy)
                a.flush()
                ctx.dinc(a.dma_start(
                    out=qn_o[128 * fc:128 * (fc + 1), 512 * tt:512 * (tt + 1)],
                    in_=otile[n % 2][:, :]), f"dqn{n % 2}")
            else:
                a.activation(qpe[:, TOWN * (fc - 16) + 512 * tt:
                                 TOWN * (fc - 16) + 512 * (tt + 1)],
                             pd[:, :], AF.Copy)
        _emit_fm_matmul(ctx, block, "qb", qbTp, qmid, 12, 32, ev_qb, psums=psA4,
                        prev_tag="qa",
                        extra_twait=lambda t: ctx.wait(t, "vn_qa",
                                                       ctx.cnt["vn_qa"]))

        def ev_kva(a, fc, tt, pd):
            if fc < 4:
                a.activation(kvn[:, TOWN * fc + 512 * tt:
                                 TOWN * fc + 512 * (tt + 1)], pd[:, :], AF.Copy)
            else:
                a.activation(kpe[:, TOWN * (fc - 4) + 512 * tt:
                                 TOWN * (fc - 4) + 512 * (tt + 1)],
                             pd[:, :], AF.Copy)
        _emit_fm_matmul(ctx, block, "kva", kvaTp, x2T, 16, 20, ev_kva,
                        psums=psA4, prev_tag="qb",
                        extra_twait=lambda t: ctx.wait(t, "dT"))

        emit_fm_rmsnorm("kva", kvn, 4, lnkvc, 512.0, None, "abc_qa")

        def ev_kvb(a, fc, tt, pd):
            n = fc * 2 + tt
            dst = kn_o if fc < 16 else v1_o
            r = fc if fc < 16 else fc - 16
            if n < 2:
                a.wait_ge(ctx.sem(f"dqn{n % 2}"), ctx.cnt[f"dqn{n % 2}"])
            if n >= 2:
                a.wait_ge(ctx.sem(f"dkvb{n % 2}"), 16 * ((n - 2) // 2 + 1))
            a.activation(otile[n % 2][:, :], pd[:, :], AF.Copy)
            a.flush()
            ctx.dinc(a.dma_start(
                out=dst[128 * r:128 * (r + 1), 512 * tt:512 * (tt + 1)],
                in_=otile[n % 2][:, :]), f"dkvb{n % 2}")
        _emit_fm_matmul(ctx, block, "kvb", kvbTp, kvn, 4, 32, ev_kvb, psums=psA4,
                        prev_tag="kva",
                        extra_twait=lambda t: ctx.wait(t, "vn_kva",
                                                       ctx.cnt["vn_kva"]))

        for which, (buf, outd) in enumerate([(qpe, qp_o), (kpe, kp_o)]):
            def v_ro(v, buf=buf, which=which):
                v = ctx.ch(v, "cV")
                if which == 0:
                    ctx.wait(v, "ev_qb")
                else:
                    ctx.wait(v, "ev_kva")
                ctx.wait(v, "aRT3")
                for hc in range(16):
                    sl = slice(TOWN * hc, TOWN * (hc + 1))
                    v.tensor_scalar_mul(rot[0:32, :], buf[32:64, sl], -1.0)
                    v.tensor_copy(rot[32:64, :], buf[0:32, sl])
                    v.tensor_mul(buf[0:64, sl], buf[0:64, sl], cosT[0:64, :])
                    v.tensor_mul(rot[0:64, :], rot[0:64, :], sinT[0:64, :])
                    v.tensor_add(buf[0:64, sl], buf[0:64, sl], rot[0:64, :])
                    v.mark(f"vRo{which}")
            block.vector(v_ro)

            def sy_ro(sync, buf=buf, outd=outd, which=which):
                for hc in range(16):
                    ctx.wait(sync, f"vRo{which}", hc + 1)
                    ctx.dinc(sync.dma_start(
                        out=outd[128 * hc:128 * (hc + 1), :],
                        in_=buf[:, TOWN * hc:TOWN * (hc + 1)]), f"dRo{which}")
            block.sync(sy_ro)

        for tg, wsrc, outd in (("spq", spqT, pq_o), ("spk", spkT, pk_o),
                               ("spv", spvT, pv_o)):
            def ev_sp(a, fc, tt, pd, outd=outd, tg=tg):
                n = fc * 2 + tt
                if n < 2:
                    prevs2 = {"spq": "dkvb", "spk": "dspq", "spv": "dspk"}
                    pn = f"{prevs2[tg]}{n % 2}"
                    a.wait_ge(ctx.sem(pn), ctx.cnt[pn])
                if n >= 2:
                    a.wait_ge(ctx.sem(f"d{tg}{n % 2}"), 16 * ((n - 2) // 2 + 1))
                a.activation(otile[n % 2][:, :], pd[:, :], AF.Copy)
                a.flush()
                ctx.dinc(a.dma_start(
                    out=outd[128 * fc:128 * (fc + 1), 512 * tt:512 * (tt + 1)],
                    in_=otile[n % 2][:, :]), f"d{tg}{n % 2}")
            prevs = {"spq": "kvb", "spk": "spq", "spv": "spk"}
            _emit_fm_matmul(ctx, block, tg, wsrc, x2T, 16, 16, ev_sp,
                            psums=psA4, prev_tag=prevs[tg])

        GPS = {}

        def t_g(t):
            ctx.wait(t, "mm_spv", ctx.cnt["mm_spv"])
            ctx.wait(t, "abc_kva", ctx.cnt["abc_kva"])
            ctx.wait(t, "aRT3", ctx.cnt["aRT3"])
            t.wait_ge(ctx.sem("dgw"), 32)
            pds = {(0, 0): psR1[:, :], (0, 1): psBc[0:1, 0:512],
                   (1, 0): psR2[:, :], (1, 1): psRo[0:1, 0:512]}
            GPS.update(pds)
            for col in range(2):
                for tt in range(2):
                    pd = pds[(col, tt)]
                    for kc in range(16):
                        inst = t.matmul(
                            pd, wstg[col][:, kc:kc + 1],
                            x2T[:, TOWN * kc + 512 * tt:TOWN * kc + 512 * (tt + 1)],
                            start=(kc == 0), stop=(kc == 15))
                    ctx.inc(inst, f"mg{col}{tt}")
        block.tensor(t_g)

        def a_g(a):
            a = ctx.ch(a, "cA")
            for tt in range(2):
                ctx.wait(a, f"mg0{tt}", 1)
                a.activation(g0t[:, 512 * tt:512 * (tt + 1)],
                             GPS[(0, tt)][0:1, 0:512], AF.Copy)
                a.mark("ag0")
        block.scalar(a_g)

        def v_g(v):
            v = ctx.ch(v, "cV")
            ctx.wait(v, "ag0")
            for tt in range(2):
                ctx.wait(v, f"mg1{tt}", 1)
                v.tensor_sub(g0t[:, 512 * tt:512 * (tt + 1)],
                             g0t[:, 512 * tt:512 * (tt + 1)],
                             GPS[(1, tt)][0:1, 0:512])
            v.tensor_sub(bdif[:, :], gbt[:, 0:1], gbt[:, 1:2])
            v.mark("vg1")
        block.vector(v_g)

        def a_g2(a):
            a = ctx.ch(a, "cA")
            ctx.wait(a, "vg1")
            a.activation(g0t[:, :], g0t[:, :], AF.Sigmoid, bias=bdif[:, 0:1])
            a.mark("ag2")
        block.scalar(a_g2)

        def v_g2(v):
            v = ctx.ch(v, "cV")
            ctx.wait(v, "ag2")
            v.tensor_scalar(g1t[:, :], g0t[:, :], -1.0, 1.0, ALU.mult, ALU.add)
            v.mark("vg2")
        block.vector(v_g2)

        def syF(sync):
            ctx.wait(sync, "vg2")
            ctx.dinc(sync.dma_start(out=g0_o[:, :], in_=g0t[:, :]), "dgo")
            ctx.dinc(sync.dma_start(out=g1_o[:, :], in_=g1t[:, :]), "dgo")
            for s_ in ("dqn0", "dqn1", "dkvb0", "dkvb1", "dRo0", "dRo1",
                       "dspq0", "dspq1", "dspk0", "dspk1", "dspv0", "dspv1",
                       "dgo"):
                ctx.wait(sync, s_)
        block.sync(syF)

    return nc


def launchF1_inputs(inputs, x2full):
    H, DN, DR = 16, 128, 64
    qaT = np.ascontiguousarray(inputs["L1_attn_q_a_w"].T)
    lnqa = np.ascontiguousarray(
        inputs["L1_attn_q_a_ln_w"].reshape(12, 128).T.astype(np.float32))
    qbT = inputs["L1_attn_q_b_w"].T
    qbTp = np.zeros((1536, 4096), np.float32)
    qc = qbT.reshape(1536, H, 192)
    for h in range(H):
        qbTp[:, 128 * h:128 * (h + 1)] = qc[:, h, 0:128]
        qbTp[:, 2048 + 128 * h:2048 + 128 * h + 64] = qc[:, h, 128:192]
    kvaT = inputs["L1_attn_kv_a_w"].T
    kvaTp = np.zeros((2048, 2560), np.float32)
    kvaTp[:, 0:512] = kvaT[:, 0:512]
    kp = kvaT[:, 512:].reshape(2048, H, DR)
    for h in range(H):
        kvaTp[:, 512 + 128 * h:512 + 128 * h + 64] = kp[:, h]
    lnkv = np.ascontiguousarray(
        inputs["L1_attn_kv_a_ln_w"].reshape(4, 128).T.astype(np.float32))
    kvbT = inputs["L1_attn_kv_b_w"].T
    kvbTp = np.zeros((512, 4096), np.float32)
    kc_ = kvbT.reshape(512, H, 256)
    for h in range(H):
        kvbTp[:, 128 * h:128 * (h + 1)] = kc_[:, h, 0:128]
        kvbTp[:, 2048 + 128 * h:2048 + 128 * (h + 1)] = kc_[:, h, 128:256]
    spqT = np.ascontiguousarray(inputs["L1_attn_sp_q_w"].T)
    spkT = np.ascontiguousarray(inputs["L1_attn_sp_k_w"].T)
    spvT = np.ascontiguousarray(inputs["L1_attn_sp_v_w"].T)
    gw = inputs["L1_attn_gate_w"]
    gw0 = np.ascontiguousarray(gw[0].reshape(16, 128).T.astype(np.float32))
    gw1 = np.ascontiguousarray(gw[1].reshape(16, 128).T.astype(np.float32))
    gb = np.ascontiguousarray(inputs["L1_attn_gate_b"].reshape(1, 2)
                              .astype(np.float32))
    invf = np.ascontiguousarray(
        (1.0 / 10000.0 ** (np.arange(0, 64, 2) / 64.0)).reshape(32, 1)
        .astype(np.float32))
    maps = []
    for c in range(NCORES):
        b, hf = c // 2, c % 2
        sl = slice(2048 * b + 1024 * hf, 2048 * b + 1024 * hf + 1024)
        posr = np.ascontiguousarray(
            (np.arange(1024) + 1024 * hf).reshape(1, 1024).astype(np.float32))
        maps.append(dict(x2own=np.ascontiguousarray(x2full[sl]), qaT=qaT,
                         lnqa=lnqa, qbTp=qbTp, kvaTp=kvaTp, lnkv=lnkv,
                         kvbTp=kvbTp, spqT=spqT, spkT=spkT, spvT=spvT,
                         gw0=gw0, gw1=gw1, gb=gb, posr=posr, invf=invf))
    return maps



def build_launchF2():
    nc = bass.Bass("TRN2", target_bir_lowering=False)
    qn = nc.dram_tensor("qn", [2048, TOWN], BF16, kind="ExternalInput").ap()
    qp = nc.dram_tensor("qp", [2048, TOWN], BF16, kind="ExternalInput").ap()
    knf = nc.dram_tensor("knf", [2048, 2048], BF16, kind="ExternalInput").ap()
    kpf = nc.dram_tensor("kpf", [2048, 2048], BF16, kind="ExternalInput").ap()
    vtm = nc.dram_tensor("vtm", [16, 2048, 128], BF16, kind="ExternalInput").ap()
    pq = nc.dram_tensor("pq", [2048, TOWN], BF16, kind="ExternalInput").ap()
    pkf = nc.dram_tensor("pkf", [2048, 2048], BF16, kind="ExternalInput").ap()
    pvtm = nc.dram_tensor("pvtm", [16, 2048, 128], BF16, kind="ExternalInput").ap()
    g0c = nc.dram_tensor("g0c", [128, 8], F32, kind="ExternalInput").ap()
    g1c = nc.dram_tensor("g1c", [128, 8], F32, kind="ExternalInput").ap()
    x2own = nc.dram_tensor("x2own", [TOWN, D], F32, kind="ExternalInput").ap()
    owT = nc.dram_tensor("owT", [D, D], F32, kind="ExternalInput").ap()
    spoT = nc.dram_tensor("spoT", [D, D], F32, kind="ExternalInput").ap()
    x3o = nc.dram_tensor("x3o", [TOWN, D], F32, kind="ExternalOutput").ap()

    SC_M = float(192.0 ** -0.5)
    SC_P = float(128.0 ** -0.5)

    es = contextlib.ExitStack()
    with es:
        ctx = Ctx(nc, es)
        sb = lambda n, s, d: es.enter_context(nc.sbuf_tensor(n, s, d))

        ctxb = sb("ctxb", [128, 16 * TOWN], BF16)
        pctxb = sb("pctxb", [128, 16 * TOWN], BF16)
        hbuf = sb("hbuf", [128, 8 * D], F32)
        qh = sb("qh", [128, TOWN], BF16)
        qph = sb("qph", [128, TOWN], BF16)
        knh = sb("knh", [128, 2048], BF16)
        kph = sb("kph", [128, 2048], BF16)
        vth = sb("vth", [128, 16 * 128], BF16)
        ptb = [sb(f"ptb{i}", [128, 512], BF16) for i in range(2)]
        scrA = sb("scrA", [128, 2048], BF16)
        qrow = sb("qrow", [1, TOWN], F32)
        kmax = sb("kmax", [1, 1], F32)
        crow = sb("crow", [1, TOWN], F32)
        lrow = sb("lrow", [1, TOWN], F32)
        linvbc = sb("linvbc", [128, TOWN], F32)
        onesbA = sb("onesbA", [128, 1], BF16)
        onesfA = sb("onesfA", [1, 128], F32)
        foldK = sb("foldK", [128, 128], BF16)
        foldQ = sb("foldQ", [128, TOWN], BF16)
        g0t = sb("g0tF", [128, 8], F32)
        g1t = sb("g1tF", [128, 8], F32)
        wo = sb("wo", [128, 16 * 512], BF16)
        x2t = [sb(f"x2t{i}", [128, D], F32) for i in range(2)]
        scrP = scrA
        sumtF = sb("sumtF", [128, 8], F32)
        invtF = sb("invtF", [128, 8], F32)
        msqF = sb("msqF", [128, 8], F32)

        psS = [es.enter_context(nc.psum_tensor(f"psS{i}", [128, 512], F32))
               for i in range(2)]
        psC = [es.enter_context(nc.psum_tensor(f"psC{i}", [128, 512], F32))
               for i in range(2)]
        psLq = es.enter_context(nc.psum_tensor("psLq", [1, TOWN], F32))
        psB3 = es.enter_context(nc.psum_tensor("psB3", [128, 512], F32))

        block = es.enter_context(nc.Block())

        for s_ in ("aex", "mlc", "msc", "abc3", "tbc3", "hdone", "aout1",
                   "aout2", "vout", "mout", "aq2", "tq1", "aqr", "ak2", "tk1",
                   "akh", "vkm", "akm", "vcr", "alr", "vlr"):
            ctx.sem(s_)

        def gp0(gp):
            g = ctx.ch(gp, "cG")
            g.memset(onesbA[:, :], 1.0)
            g.memset(onesfA[:, :], 1.0)
            g.memset(foldK[:, :], 0.0)
            g.memset(foldK[0:1, :], 1.0)
            g.memset(foldQ[:, :], 0.0)
            g.mark("gE")
        block.gpsimd(gp0)

        def sy0(sync):
            ctx.dinc(sync.dma_start(out=g0t[:, :], in_=g0c[:, :]), "dS")
            ctx.dinc(sync.dma_start(out=g1t[:, :], in_=g1c[:, :]), "dS")
        block.sync(sy0)

        for which in range(2):
            scale = SC_M if which == 0 else SC_P
            dstb = ctxb if which == 0 else pctxb
            for h in range(16):
                hid = which * 16 + h

                def syh(sync, h=h, which=which, hid=hid):
                    if hid >= 1:
                        ctx.wait(sync, "hdone", hid)
                    srcs = ([(qh, qn), (qph, qp), (knh, knf), (kph, kpf)]
                            if which == 0 else [(qh, pq), (knh, pkf)])
                    for dst_, src_ in srcs:
                        ctx.dinc(sync.dma_start(
                            out=dst_[:, :], in_=src_[128 * h:128 * (h + 1), :]),
                            "dqh")
                    vsrc = vtm if which == 0 else pvtm
                    ctx.dinc(sync.dma_start(
                        out=vth[:, :].rearrange("p (c d) -> p c d", d=128),
                        in_=vsrc[h].rearrange("(c p) d -> p c d", p=128)),
                        "dqh")
                block.sync(syh)

                def ah1(a, which=which, hid=hid):
                    a = ctx.ch(a, "cA")
                    a.wait_ge(ctx.sem("dqh"), ctx.cnt["dqh"])
                    if hid >= 1:
                        ctx.wait(a, "hdone", hid)
                    a.activation(scrA[:, 0:TOWN], qh[:, :], AF.Square)
                    if which == 0:
                        a.activation(scrA[:, TOWN:2 * TOWN], qph[:, :],
                                     AF.Square)
                    a.mark("aq2")
                block.scalar(ah1)

                def th1(t, which=which, hid=hid):
                    if hid == 0:
                        ctx.wait(t, "gE")
                    ctx.wait(t, "aq2", ctx.cnt["aq2"])
                    for tt in range(2):
                        inst = t.matmul(psLq[:, 512 * tt:512 * (tt + 1)],
                                        onesbA[:, :],
                                        scrA[:, 512 * tt:512 * (tt + 1)],
                                        start=True, stop=(which == 1))
                        if which == 0:
                            inst = t.matmul(psLq[:, 512 * tt:512 * (tt + 1)],
                                            onesbA[:, :],
                                            scrA[:, TOWN + 512 * tt:
                                                 TOWN + 512 * (tt + 1)],
                                            start=False, stop=True)
                        ctx.inc(inst, "tq1")
                block.tensor(th1)

                def ah2(a, which=which):
                    a = ctx.ch(a, "cA")
                    ctx.wait(a, "tq1", ctx.cnt["tq1"])
                    a.activation(qrow[:, :], psLq[0:1, :], AF.Sqrt)
                    a.mark("aqr")
                block.scalar(ah2)

                for half in range(2):
                    def ah2b(a, which=which, half=half):
                        a = ctx.ch(a, "cA")
                        if half == 0:
                            ctx.wait(a, "aqr", ctx.cnt["aqr"])
                        else:
                            ctx.wait(a, "akh", ctx.cnt["akh"])
                        a.activation(scrA[:, 0:TOWN],
                                     knh[:, TOWN * half:TOWN * (half + 1)],
                                     AF.Square)
                        if which == 0:
                            a.activation(scrA[:, TOWN:2 * TOWN],
                                         kph[:, TOWN * half:TOWN * (half + 1)],
                                         AF.Square)
                        a.mark("ak2")
                    block.scalar(ah2b)

                    def th2(t, which=which, half=half):
                        ctx.wait(t, "ak2", ctx.cnt["ak2"])
                        for tt in range(2):
                            inst = t.matmul(psLq[:, 512 * tt:512 * (tt + 1)],
                                            onesbA[:, :],
                                            scrA[:, 512 * tt:512 * (tt + 1)],
                                            start=True, stop=(which == 1))
                            if which == 0:
                                inst = t.matmul(psLq[:, 512 * tt:512 * (tt + 1)],
                                                onesbA[:, :],
                                                scrA[:, TOWN + 512 * tt:
                                                     TOWN + 512 * (tt + 1)],
                                                start=False, stop=True)
                            ctx.inc(inst, "tk1")
                    block.tensor(th2)

                    def ah3(a, which=which, half=half):
                        a = ctx.ch(a, "cA")
                        ctx.wait(a, "tk1", ctx.cnt["tk1"])
                        ctx.wait(a, "vkm", ctx.cnt["vkm"])
                        a.activation(lrow[:, :], psLq[0:1, :], AF.Copy)
                        a.mark("akh")
                    block.scalar(ah3)

                    def vh0(v, which=which, half=half):
                        v = ctx.ch(v, "cV")
                        ctx.wait(v, "akh", ctx.cnt["akh"])
                        if half == 0:
                            v.reduce_max(kmax[:, :], lrow[:, :], AX.X)
                        else:
                            v.reduce_max(crow[:, 0:1], lrow[:, :], AX.X)
                            v.tensor_max(kmax[:, :], kmax[:, :], crow[:, 0:1])
                        v.mark("vkm")
                    block.vector(vh0)

                def ah3b(a):
                    a = ctx.ch(a, "cA")
                    ctx.wait(a, "vkm", ctx.cnt["vkm"])
                    a.activation(kmax[:, :], kmax[:, :], AF.Sqrt)
                    a.mark("akm")
                block.scalar(ah3b)

                def vh2(v, which=which, scale=scale):
                    v = ctx.ch(v, "cV")
                    ctx.wait(v, "akm", ctx.cnt["akm"])
                    v.tensor_scalar_mul(crow[:, :], qrow[:, :], kmax[:, 0:1])
                    v.tensor_scalar(crow[:, :], crow[:, :], -1.0, 60.0 / scale,
                                    ALU.mult, ALU.add)
                    if which == 0:
                        v.tensor_copy(qph[64:65, :], crow[:, :])
                        v.memset(kph[64:65, :], 1.0)
                    else:
                        v.tensor_copy(foldQ[0:1, :], crow[:, :])
                    v.mark("vcr")
                block.vector(vh2)

                def th34(t, which=which, hid=hid):
                    ctx.wait(t, "vcr", ctx.cnt["vcr"])
                    aex0 = ctx.cnt["aex"]

                    def lctx(n):
                        kc, qt = n // 2, n % 2
                        t.matmul(psLq[:, 512 * qt:512 * (qt + 1)],
                                 onesbA[:, :], ptb[n % 2][:, :],
                                 start=(kc == 0), stop=(kc == 15))
                        inst2 = t.matmul(psC[qt][:, :],
                                         vth[:, 128 * kc:128 * (kc + 1)],
                                         ptb[n % 2][:, :],
                                         start=(kc == 0), stop=(kc == 15))
                        ctx.inc(inst2, "mlc")

                    for n in range(32):
                        kc, qt = n // 2, n % 2
                        if n >= 2:
                            ctx.wait(t, "aex", aex0 + n - 1)
                        pd = psS[n % 2]
                        t.matmul(pd[:, :], knh[:, 128 * kc:128 * (kc + 1)],
                                 qh[:, 512 * qt:512 * (qt + 1)],
                                 start=True, stop=False)
                        if which == 0:
                            inst = t.matmul(pd[:, :],
                                            kph[:, 128 * kc:128 * (kc + 1)],
                                            qph[:, 512 * qt:512 * (qt + 1)],
                                            start=False, stop=True)
                        else:
                            inst = t.matmul(pd[:, :], foldK[:, :],
                                            foldQ[:, 512 * qt:512 * (qt + 1)],
                                            start=False, stop=True)
                        ctx.inc(inst, "msc")
                        if n >= 1:
                            ctx.wait(t, "aex", aex0 + n)
                            lctx(n - 1)
                    ctx.wait(t, "aex", aex0 + 32)
                    lctx(31)
                block.tensor(th34)

                def ah4(a, scale=scale):
                    a = ctx.ch(a, "cA")
                    msc0 = ctx.cnt["msc"] - 32
                    mlc0 = ctx.cnt["mlc"] - 32
                    for kc in range(16):
                        for qt in range(2):
                            n = kc * 2 + qt
                            ctx.wait(a, "msc", msc0 + n + 1)
                            if n >= 2:
                                ctx.wait(a, "mlc", mlc0 + n - 1)
                            a.activation(ptb[n % 2][:, :], psS[n % 2][:, :],
                                         AF.Exp, scale=scale)
                            a.mark("aex")
                block.scalar(ah4)

                def ah5(a):
                    a = ctx.ch(a, "cA")
                    ctx.wait(a, "mlc", ctx.cnt["mlc"])
                    a.activation(lrow[:, :], psLq[0:1, :], AF.Copy)
                    a.mark("alr")
                block.scalar(ah5)

                def vh3(v):
                    v = ctx.ch(v, "cV")
                    ctx.wait(v, "alr", ctx.cnt["alr"])
                    v.reciprocal(lrow[:, :], lrow[:, :])
                    v.mark("vlr")
                block.vector(vh3)

                def th5(t):
                    ctx.wait(t, "vlr", ctx.cnt["vlr"])
                    abc0 = ctx.cnt["abc3"]
                    for tt in range(2):
                        if tt == 1:
                            ctx.wait(t, "abc3", abc0 + 1)
                        ctx.inc(t.matmul(psB3[:, :], onesfA[:, :],
                                         lrow[:, 512 * tt:512 * (tt + 1)],
                                         start=True, stop=True), "tbc3")
                block.tensor(th5)

                def ah6(a):
                    a = ctx.ch(a, "cA")
                    tb0 = ctx.cnt["tbc3"] - 2
                    for tt in range(2):
                        ctx.wait(a, "tbc3", tb0 + tt + 1)
                        a.activation(linvbc[:, 512 * tt:512 * (tt + 1)],
                                     psB3[:, :], AF.Copy)
                        a.mark("abc3")
                block.scalar(ah6)

                def vh4(v, h=h, dstb=dstb):
                    v = ctx.ch(v, "cV")
                    ctx.wait(v, "abc3", ctx.cnt["abc3"])
                    for qt in range(2):
                        v.tensor_mul(dstb[:, TOWN * h + 512 * qt:
                                          TOWN * h + 512 * (qt + 1)],
                                     psC[qt][:, :],
                                     linvbc[:, 512 * qt:512 * (qt + 1)])
                    v.mark("hdone")
                block.vector(vh4)

        def gpw2(gp):
            ctx.wait(gp, "hdone", 32)
            for half in range(2):
                wsrc = owT if half == 0 else spoT
                for ft in range(4):
                    n = half * 4 + ft
                    if n >= 1:
                        ctx.wait(gp, "mout", 8 * n)
                    ctx.dinc(gp.dma_start(
                        out=wo[:, :].rearrange("p (c f) -> p c f", f=512),
                        in_=wsrc.rearrange("(c p) f -> p c f", p=128)
                            [:, :, 512 * ft:512 * (ft + 1)]), "dwo")
        block.gpsimd(gpw2)

        def tout(t):
            ctx.wait(t, "hdone", 32)
            for half in range(2):
                srcb = ctxb if half == 0 else pctxb
                gmark = "aout1" if half == 0 else "aout2"
                for ft in range(4):
                    n = half * 4 + ft
                    t.wait_ge(ctx.sem("dwo"), 16 * (n + 1))
                    for tc in range(8):
                        m = ft * 8 + tc
                        if m >= 2:
                            ctx.wait(t, gmark, m - 1)
                        pd = psS[m % 2] if half == 0 else psC[m % 2]
                        for hc in range(16):
                            inst = t.matmul(
                                pd[:, :],
                                srcb[:, TOWN * hc + 128 * tc:
                                     TOWN * hc + 128 * (tc + 1)],
                                wo[:, 512 * hc:512 * (hc + 1)],
                                start=(hc == 0), stop=(hc == 15))
                        ctx.inc(inst, "mout")
        block.tensor(tout)

        def vev(v):
            v = ctx.ch(v, "cV")
            ctx.wait(v, "dS")
            for ft in range(4):
                for tc in range(8):
                    m = ft * 8 + tc
                    ctx.wait(v, "mout", m + 1)
                    v.tensor_scalar_mul(
                        hbuf[:, D * tc + 512 * ft:D * tc + 512 * (ft + 1)],
                        psS[m % 2][:, :], g0t[:, tc:tc + 1])
                    v.mark("aout1")
            for ft in range(4):
                for tc in range(8):
                    m = ft * 8 + tc
                    ctx.wait(v, "mout", 32 + m + 1)
                    v.scalar_tensor_tensor(
                        hbuf[:, D * tc + 512 * ft:D * tc + 512 * (ft + 1)],
                        psC[m % 2][:, :], g1t[:, tc:tc + 1],
                        hbuf[:, D * tc + 512 * ft:D * tc + 512 * (ft + 1)],
                        ALU.mult, ALU.add)
                    v.mark("aout2")
        block.vector(vev)

        def syx2(sync):
            for i in range(8):
                if i >= 2:
                    ctx.wait(sync, "vout", i - 1)
                ctx.dinc(sync.dma_start(out=x2t[i % 2][:, :],
                                        in_=x2own[128 * i:128 * (i + 1), :]),
                         f"dx2{i % 2}")
        block.sync(syx2)

        def vres(v):
            v = ctx.ch(v, "cV")
            ctx.wait(v, "aout2")
            for tc in range(8):
                v.wait_ge(ctx.sem(f"dx2{tc % 2}"), 16 * (tc // 2 + 1))
                v.tensor_add(hbuf[:, D * tc:D * (tc + 1)],
                             hbuf[:, D * tc:D * (tc + 1)], x2t[tc % 2][:, :])
                v.mark("vout")
        block.vector(vres)

        _emit_postln_store(ctx, block, hbuf, scrP, sumtF, invtF, msqF, x3o,
                           prewait="vout")
    return nc


def launchF2_inputs(inputs, x2full, f1res):
    owT = np.ascontiguousarray(inputs["L1_attn_o_w"].T)
    spoT = np.ascontiguousarray(inputs["L1_attn_sp_o_w"].T)
    maps = []
    for c in range(NCORES):
        b, hf = c // 2, c % 2
        r0, r1 = f1res[2 * b], f1res[2 * b + 1]
        me = f1res[c]
        knf = np.concatenate([r0["kn_o"], r1["kn_o"]], axis=1)
        kpf = np.concatenate([r0["kp_o"], r1["kp_o"]], axis=1)
        pkf = np.concatenate([r0["pk_o"], r1["pk_o"]], axis=1)
        vfull = np.concatenate([r0["v1_o"], r1["v1_o"]], axis=1)
        pvfull = np.concatenate([r0["pv_o"], r1["pv_o"]], axis=1)
        vtm = np.stack([np.ascontiguousarray(vfull[128 * h:128 * (h + 1), :].T)
                        for h in range(16)])
        pvtm = np.stack([np.ascontiguousarray(pvfull[128 * h:128 * (h + 1), :].T)
                         for h in range(16)])
        g0c = np.ascontiguousarray(me["g0_o"][0].reshape(8, 128).T)
        g1c = np.ascontiguousarray(me["g1_o"][0].reshape(8, 128).T)
        sl = slice(2048 * b + 1024 * hf, 2048 * b + 1024 * hf + 1024)
        maps.append(dict(qn=me["qn_o"], qp=me["qp_o"], knf=knf, kpf=kpf,
                         vtm=vtm, pq=me["pq_o"], pkf=pkf, pvtm=pvtm,
                         g0c=g0c, g1c=g1c,
                         x2own=np.ascontiguousarray(x2full[sl]),
                         owT=owT, spoT=spoT))
    return maps



def build_launchG1():
    nc = bass.Bass("TRN2", target_bir_lowering=False)
    x3f = nc.dram_tensor("x3f", [8192, D], F32, kind="ExternalInput").ap()
    wgu = nc.dram_tensor("wgu", [2048, D], F32, kind="ExternalInput").ap()
    wps = nc.dram_tensor("wps", [D, 1024], F32, kind="ExternalInput").ap()
    partg = nc.dram_tensor("partg", [D, 8192], F32, kind="ExternalOutput").ap()
    ssqo = nc.dram_tensor("ssqo", [1, 8192], F32, kind="ExternalOutput").ap()
    x3bf = nc.dram_tensor("x3bf", [8192, D], BF16).ap()
    wgubf = nc.dram_tensor("wgubf", [2048, D], BF16).ap()
    wpsbf = nc.dram_tensor("wpsbf", [D, 1024], BF16).ap()
    wguT = nc.dram_tensor("wguT", [D, 2048], BF16).ap()
    wpsT = nc.dram_tensor("wpsT", [1024, D], BF16).ap()

    es = contextlib.ExitStack()
    with es:
        ctx = Ctx(nc, es)
        sb = lambda n, s, d: es.enter_context(nc.sbuf_tensor(n, s, d))

        wch = [sb(f"wch{i}", [128, D], BF16) for i in range(2)]
        ssg = sb("ssg", [128, 32], F32)
        ssgb = sb("ssgb", [128, 32], BF16)
        rsc = sb("rsc", [128, 32], F32)
        tq = sb("tq", [128, D], BF16)
        rnd = sb("rnd", [128, D], F32)
        wTs2 = [sb(f"wTs2{i}", [128, 16 * 128], BF16) for i in range(2)]
        identg = sb("identg", [128, 128], BF16)
        x3T = sb("x3T", [128, 16 * 1024], BF16)
        hFM = sb("hFM", [128, 8 * 1024], BF16)
        silg = sb("silg", [128, 512], BF16)
        scrg = sb("scrg", [128, 1024], BF16)
        onesg = sb("onesg", [128, 1], BF16)
        ssqrow = sb("ssqrow", [1, 1024], F32)
        wstr2 = [sb(f"wstr2{i}", [128, 8 * 128], BF16) for i in range(2)]
        outg = [sb(f"outg{i}", [128, 512], F32) for i in range(2)]

        psq = [es.enter_context(nc.psum_tensor(f"psq{i}", [128, 512], F32))
               for i in range(4)]
        psT3 = [es.enter_context(nc.psum_tensor(f"psT3{i}", [128, 128], BF16))
                for i in range(2)]
        psr3 = es.enter_context(nc.psum_tensor("psr3", [1, 1024], F32))

        block = es.enter_context(nc.Block())
        for s_ in ("vq", "tq3", "aq3", "mgu3", "egu3", "asil0", "asil3",
                   "mpr3", "epr3", "assq", "tsq3", "dwt0", "dwt1", "dssq",
                   "dpg0", "dpg1"):
            ctx.sem(s_)

        def gp0(gp):
            g = ctx.ch(gp, "cG")
            g.memset(onesg[:, :], 1.0)
            make_identity(g, identg[:, :])
            g.mark("gI3")
            ctx.dinc(gp.dma_start(out=wgubf[:, :], in_=wgu[:, :]), "dCst")
            ctx.dinc(gp.dma_start(out=wpsbf[:, :], in_=wps[:, :]), "dCst")
            ctx.dinc(gp.dma_start(out=x3bf[:, :], in_=x3f[:, :]), "dXc")
        block.gpsimd(gp0)

        def syq(sync):
            ctx.wait(sync, "dCst")
            for wi in range(32):
                ch = wi % 16
                if wi >= 2:
                    ctx.wait(sync, "vq", 11 * (wi - 1))
                if wi < 16:
                    ctx.dinc(sync.dma_start(out=wch[wi % 2][:, :],
                                            in_=wgubf[128 * ch:128 * (ch + 1),
                                                      :]),
                             f"dq{wi % 2}")
                else:
                    ctx.dinc(sync.dma_start(out=wch[wi % 2][:, 0:1024],
                                            in_=wpsbf[128 * ch:128 * (ch + 1),
                                                      :]),
                             f"dq{wi % 2}")
        block.sync(syq)

        for wi in range(32):
            W = 2048 if wi < 16 else 1024
            NG = W // 64
            NJ = W // 128

            def vq(v, wi=wi, W=W, NG=NG):
                v = ctx.ch(v, "cV")
                v.wait_ge(ctx.sem(f"dq{wi % 2}"), 16 * (wi // 2 + 1))
                if wi >= 1:
                    ctx.wait(v, "tq3", ctx.cnt["tq3"])
                w_ = wch[wi % 2]
                v.tensor_reduce(ssg[:, 0:NG],
                                w_[:, 0:W].rearrange("p (a b) -> p a b", b=64),
                                AX.X, ALU.add, apply_absolute_value=True)
                v.tensor_scalar(ssgb[:, 0:NG], ssg[:, 0:NG], 1.0 / 64.0, 1e-8,
                                ALU.mult, ALU.max)
                v.tensor_copy(rsc[:, 0:NG], ssgb[:, 0:NG])
                v.reciprocal(rsc[:, 0:NG], rsc[:, 0:NG])
                v.tensor_mul(tq[:, 0:W], w_[:, 0:W],
                             rsc[:, 0:NG].unsqueeze(2).broadcast_to(
                                 [128, NG, 64]))
                v.tensor_scalar(rnd[:, 0:W], tq[:, 0:W], 12582912.0,
                                12582912.0, ALU.add, ALU.subtract)
                v.tensor_scalar(tq[:, 0:W], rnd[:, 0:W], -1.0, 1.0,
                                ALU.max, ALU.min)
                v.tensor_mul(tq[:, 0:W], tq[:, 0:W],
                             ssgb[:, 0:NG].unsqueeze(2).broadcast_to(
                                 [128, NG, 64]))
                v.tensor_sub(tq[:, 0:W], tq[:, 0:W], w_[:, 0:W])
                v.tensor_add(tq[:, 0:W], tq[:, 0:W], w_[:, 0:W])
                v.mark("vq", 11)
            block.vector(vq)

            def tq3f(t, wi=wi, NJ=NJ):
                ctx.wait(t, "vq", ctx.cnt["vq"])
                if wi == 0:
                    ctx.wait(t, "gI3")
                for j in range(NJ):
                    n = ctx.cnt["tq3"]
                    if n >= 2:
                        ctx.wait(t, "aq3", n - 1)
                    ctx.inc(t.transpose(psT3[n % 2][:, :],
                                        tq[:, 128 * j:128 * (j + 1)],
                                        identg[:, :]), "tq3")
            block.tensor(tq3f)

            def aq3f(a, wi=wi, NJ=NJ):
                a = ctx.ch(a, "cA")
                if wi >= 2:
                    a.wait_ge(ctx.sem(f"dwt{wi % 2}"), 16 * ((wi - 2) // 2 + 1))
                base = ctx.cnt["tq3"] - NJ
                for j in range(NJ):
                    ctx.wait(a, "tq3", base + j + 1)
                    a.activation(wTs2[wi % 2][:, 128 * j:128 * (j + 1)],
                                 psT3[(base + j) % 2][:, :], AF.Copy)
                    a.mark("aq3")
            block.scalar(aq3f)

            def gpq(gp, wi=wi, NJ=NJ):
                ctx.wait(gp, "aq3", ctx.cnt["aq3"])
                ch = wi % 16
                if wi < 16:
                    dst = wguT.rearrange("(c p) n -> p c n", p=128)[
                        :, :, 128 * ch:128 * (ch + 1)]
                else:
                    dst = wpsT.rearrange("(c p) n -> p c n", p=128)[
                        :, :, 128 * ch:128 * (ch + 1)]
                ctx.dinc(gp.dma_start(
                    out=dst,
                    in_=wTs2[wi % 2][:, 0:NJ * 128]
                        .rearrange("p (c n) -> p c n", n=128)), f"dwt{wi % 2}")
            block.gpsimd(gpq)

        def gu_done(n):
            te, pos = n // 16, n % 16
            j, half = pos // 2, pos % 2
            return 4 * (te * 8 + j) + (3 if half == 0 else 4)

        def syX3(sync):
            ctx.wait(sync, "dXc")
            for te in range(8):
                if te >= 1:
                    ctx.wait(sync, "mgu3", 32 * te)
                for j in range(16):
                    ctx.dinc(sync.dma_start(
                        out=x3T[:, 1024 * j:1024 * (j + 1)],
                        in_=x3bf[1024 * te:1024 * (te + 1),
                                 128 * j:128 * (j + 1)],
                        transpose=True), f"dX3{te % 2}")
        block.sync(syX3)

        def gpwg(gp):
            ctx.wait(gp, "dwt0")
            ctx.wait(gp, "dwt1")
            for te in range(8):
                for pos in range(16):
                    n = te * 16 + pos
                    j, half = pos // 2, pos % 2
                    jc = j if half == 0 else 8 + j
                    if n >= 2:
                        ctx.wait(gp, "mgu3", gu_done(n - 2))
                    ctx.dinc(gp.dma_start(
                        out=wTs2[n % 2][:, :].rearrange("p (c f) -> p c f",
                                                        f=128),
                        in_=wguT.rearrange("(c p) n -> p c n", p=128)
                            [:, :, 128 * jc:128 * (jc + 1)]), f"dws3{n % 2}")
                for fc in range(16):
                    n = te * 16 + fc
                    if n >= 2:
                        ctx.wait(gp, "mpr3", 2 * (n - 1))
                    ctx.dinc(gp.dma_start(
                        out=wstr2[n % 2][:, :].rearrange("p (c f) -> p c f",
                                                         f=128),
                        in_=wpsT.rearrange("(c p) n -> p c n", p=128)
                            [:, :, 128 * fc:128 * (fc + 1)]), f"dwp3{n % 2}")
        block.gpsimd(gpwg)

        def tmm3(t):
            for te in range(8):
                t.wait_ge(ctx.sem(f"dX3{te % 2}"), 256 * (te // 2 + 1))
                if te >= 1:
                    ctx.wait(t, "epr3", 32 * te)
                for j in range(8):
                    ng = te * 16 + 2 * j
                    nu = te * 16 + 2 * j + 1
                    t.wait_ge(ctx.sem(f"dws3{ng % 2}"), 16 * (ng // 2 + 1))
                    t.wait_ge(ctx.sem(f"dws3{nu % 2}"), 16 * (nu // 2 + 1))
                    for tt in range(2):
                        k = (te * 8 + j) * 2 + tt
                        if k >= 2:
                            ctx.wait(t, "egu3", k - 1)
                        for kc in range(16):
                            ig = t.matmul(
                                psq[2 * (k % 2)][:, :],
                                wTs2[ng % 2][:, 128 * kc:128 * (kc + 1)],
                                x3T[:, 1024 * kc + 512 * tt:
                                    1024 * kc + 512 * (tt + 1)],
                                start=(kc == 0), stop=(kc == 15))
                        ctx.inc(ig, "mgu3")
                        for kc in range(16):
                            iu = t.matmul(
                                psq[2 * (k % 2) + 1][:, :],
                                wTs2[nu % 2][:, 128 * kc:128 * (kc + 1)],
                                x3T[:, 1024 * kc + 512 * tt:
                                    1024 * kc + 512 * (tt + 1)],
                                start=(kc == 0), stop=(kc == 15))
                        ctx.inc(iu, "mgu3")
                ctx.wait(t, "assq", te)
                for j in range(8):
                    ctx.wait(t, "asil3", te * 8 + j + 1)
                    for tt in range(2):
                        inst = t.matmul(psr3[:, 512 * tt:512 * (tt + 1)],
                                        onesg[:, :],
                                        scrg[:, 512 * tt:512 * (tt + 1)],
                                        start=(j == 0), stop=(j == 7))
                    ctx.inc(inst, "tsq3")
                ctx.wait(t, "egu3", 16 * (te + 1))
                for fc in range(16):
                    n = te * 16 + fc
                    t.wait_ge(ctx.sem(f"dwp3{n % 2}"), 16 * (n // 2 + 1))
                    for tt in range(2):
                        d_ = n * 2 + tt
                        if d_ >= 2:
                            ctx.wait(t, "epr3", d_ - 1)
                        pd = psq[2 * (d_ % 2)]
                        for ic in range(8):
                            inst = t.matmul(
                                pd[:, :],
                                wstr2[n % 2][:, 128 * ic:128 * (ic + 1)],
                                hFM[:, 1024 * ic + 512 * tt:
                                    1024 * ic + 512 * (tt + 1)],
                                start=(ic == 0), stop=(ic == 7))
                        ctx.inc(inst, "mpr3")
        block.tensor(tmm3)

        def asil(a):
            a = ctx.ch(a, "cA")
            for te in range(8):
                for j in range(8):
                    for tt in range(2):
                        k = (te * 8 + j) * 2 + tt
                        ctx.wait(a, "mgu3", 2 * k + 1)
                        if k >= 1:
                            ctx.wait(a, "egu3", k)
                        a.activation(silg[:, :], psq[2 * (k % 2)][:, :],
                                     AF.Silu)
                        a.mark("asil0")
                for j in range(8):
                    ctx.wait(a, "egu3", 16 * te + 2 * (j + 1))
                    if te * 8 + j >= 1:
                        ctx.wait(a, "tsq3", te * 8 + j)
                    a.activation(scrg[:, :], hFM[:, 1024 * j:1024 * (j + 1)],
                                 AF.Square)
                    a.mark("asil3")
                ctx.wait(a, "tsq3", 8 * (te + 1))
                if te >= 1:
                    a.wait_ge(ctx.sem("dssq"), 16 * te)
                a.activation(ssqrow[:, :], psr3[0:1, :], AF.Copy)
                a.flush()
                ctx.dinc(a.dma_start(
                    out=ssqo[:, 1024 * te:1024 * (te + 1)],
                    in_=ssqrow[:, :]), "dssq")
                a.mark("assq")
                for fc in range(16):
                    for tt in range(2):
                        d_ = (te * 16 + fc) * 2 + tt
                        ctx.wait(a, "mpr3", d_ + 1)
                        if d_ >= 2:
                            a.wait_ge(ctx.sem(f"dpg{d_ % 2}"),
                                      16 * ((d_ - 2) // 2 + 1))
                        a.activation(outg[d_ % 2][:, :],
                                     psq[2 * (d_ % 2)][:, :], AF.Copy)
                        a.flush()
                        ctx.dinc(a.dma_start(
                            out=partg[128 * fc:128 * (fc + 1),
                                      1024 * te + 512 * tt:
                                      1024 * te + 512 * (tt + 1)],
                            in_=outg[d_ % 2][:, :]), f"dpg{d_ % 2}")
                        a.mark("epr3")
        block.scalar(asil)

        def vh3g(v):
            v = ctx.ch(v, "cV")
            for te in range(8):
                if te >= 1:
                    ctx.wait(v, "mpr3", 32 * te)
                for j in range(8):
                    for tt in range(2):
                        k = (te * 8 + j) * 2 + tt
                        ctx.wait(v, "asil0", k + 1)
                        ctx.wait(v, "mgu3", 2 * k + 2)
                        v.tensor_mul(hFM[:, 1024 * j + 512 * tt:
                                         1024 * j + 512 * (tt + 1)],
                                     silg[:, :], psq[2 * (k % 2) + 1][:, :])
                        v.mark("egu3")
        block.vector(vh3g)

        def syF3(sync):
            ctx.wait(sync, "dssq")
            ctx.wait(sync, "dpg0")
            ctx.wait(sync, "dpg1")
        block.sync(syF3)
    return nc



def build_launchG2():
    nc = bass.Bass("TRN2", target_bir_lowering=False)
    parts = nc.dram_tensor("parts", [8, D, TOWN], F32, kind="ExternalInput").ap()
    ssqs = nc.dram_tensor("ssqs", [8, 1, TOWN], F32, kind="ExternalInput").ap()
    x3own = nc.dram_tensor("x3own", [TOWN, D], F32, kind="ExternalInput").ap()
    x4o = nc.dram_tensor("x4o", [TOWN, D], F32, kind="ExternalOutput").ap()

    es = contextlib.ExitStack()
    with es:
        ctx = Ctx(nc, es)
        sb = lambda n, s, d: es.enter_context(nc.sbuf_tensor(n, s, d))
        ffacc = sb("ffaccG", [128, 16 * TOWN], F32)
        htm = sb("htmG", [128, 8 * D], F32)
        scrb_ = sb("scrG", [128, D], F32)
        sumt = sb("sumtG", [128, 8], F32)
        invt = sb("invtG", [128, 8], F32)
        msq = sb("msqG", [128, 8], F32)
        srow = sb("srowG", [1, TOWN], F32)
        icol = sb("icolG", [128, 8], F32)
        ones1G = sb("ones1G", [1, 1], F32)
        identf4 = sb("identf4", [128, 128], F32)
        psF = [es.enter_context(nc.psum_tensor(f"psFG{i}", [128, 512], F32))
               for i in range(2)]
        psc4 = es.enter_context(nc.psum_tensor("psc4", [128, 8], F32))

        block = es.enter_context(nc.Block())

        def gp0(gp):
            g = ctx.ch(gp, "cG")
            g.memset(ones1G[:, :], 1.0)
            make_identity(g, identf4[:, :])
            g.mark("gI4")
            for p in range(8):
                if p >= 1:
                    ctx.wait(gp, "dAcc")
                ctx.dinc(gp.dma_start(
                    out=ffacc[:, :].rearrange("p (c t) -> p c t", t=TOWN),
                    in_=parts[p].rearrange("(c p) t -> p c t", p=128),
                    accum_op=(ALU.add if p > 0 else ALU.bypass)), "dAcc")
            for p in range(8):
                ctx.wait(gp, "dAq")
                ctx.dinc(gp.dma_start(
                    out=srow[:, :], in_=ssqs[p],
                    accum_op=(ALU.add if p > 0 else ALU.bypass)), "dAq")
        block.gpsimd(gp0)

        def v0(v):
            v = ctx.ch(v, "cV")
            ctx.wait(v, "dAq")
            v.tensor_scalar(srow[:, :], srow[:, :], 1.0 / 8192.0,
                            float(np.finfo(np.float32).eps), ALU.mult, ALU.add)
            v.mark("vS4")
        block.vector(v0)

        def a0(a):
            a = ctx.ch(a, "cA")
            ctx.wait(a, "vS4")
            a.activation(srow[:, :], srow[:, :], AF.Ln)
            a.activation(srow[:, :], srow[:, :], AF.Exp, scale=-0.5)
            a.mark("aS4")
        block.scalar(a0)

        def t0(t):
            ctx.wait(t, "aS4")
            ctx.wait(t, "gI4")
            for tc in range(8):
                ctx.inc(t.matmul(psc4[:, tc:tc + 1],
                                 srow[:, 128 * tc:128 * (tc + 1)],
                                 ones1G[:, :], start=True, stop=True), "tC4")
        block.tensor(t0)

        def a1(a):
            a = ctx.ch(a, "cA")
            for tc in range(8):
                ctx.wait(a, "tC4", tc + 1)
                a.activation(icol[:, tc:tc + 1], psc4[:, tc:tc + 1], AF.Copy)
                a.mark("aC4")
        block.scalar(a1)

        def t1(t):
            ctx.wait(t, "dAcc")
            for fc in range(16):
                for tc in range(8):
                    n = fc * 8 + tc
                    if n >= 2:
                        ctx.wait(t, "aTr4", n - 1)
                    ctx.inc(t.transpose(psF[n % 2][:, 0:128],
                                        ffacc[:, TOWN * fc + 128 * tc:
                                              TOWN * fc + 128 * (tc + 1)],
                                        identf4[:, :]), "tTr4")
        block.tensor(t1)

        def a2(a):
            a = ctx.ch(a, "cA")
            for fc in range(16):
                for tc in range(8):
                    n = fc * 8 + tc
                    ctx.wait(a, "tTr4", n + 1)
                    a.activation(htm[:, D * tc + 128 * fc:
                                     D * tc + 128 * (fc + 1)],
                                 psF[n % 2][:, 0:128], AF.Copy)
                    a.mark("aTr4")
        block.scalar(a2)

        def sy1(sync):
            ctx.wait(sync, "tTr4")
            for tc in range(8):
                ctx.dinc(sync.dma_start(out=ffacc[:, D * tc:D * (tc + 1)],
                                        in_=x3own[128 * tc:128 * (tc + 1), :]),
                         "dX3r")
        block.sync(sy1)

        def v1(v):
            v = ctx.ch(v, "cV")
            ctx.wait(v, "aTr4")
            ctx.wait(v, "aC4")
            ctx.wait(v, "dX3r")
            for tc in range(8):
                v.scalar_tensor_tensor(htm[:, D * tc:D * (tc + 1)],
                                       htm[:, D * tc:D * (tc + 1)],
                                       icol[:, tc:tc + 1],
                                       ffacc[:, D * tc:D * (tc + 1)],
                                       ALU.mult, ALU.add)
            v.mark("vRes4")
        block.vector(v1)

        _emit_postln_store(ctx, block, htm, scrb_, sumt, invt, msq, x4o,
                           prewait="vRes4")
    return nc


def launchE1_inputs(inputs, x1full):
    maps = []
    for c in range(NCORES):
        sl = slice(2048 * c, 2048 * (c + 1))
        maps.append(dict(
            x1f=x1full,
            gwp=np.ascontiguousarray(inputs["L0_ffn_gate_W_packed"][sl]),
            gsc=np.ascontiguousarray(inputs["L0_ffn_gate_scales"][sl]),
            uwp=np.ascontiguousarray(inputs["L0_ffn_up_W_packed"][sl]),
            usc=np.ascontiguousarray(inputs["L0_ffn_up_scales"][sl]),
            dwp=np.ascontiguousarray(
                inputs["L0_ffn_down_W_packed"][:, 1024 * c:1024 * (c + 1)]),
            dsc=np.ascontiguousarray(
                inputs["L0_ffn_down_scales"][:, 64 * c:64 * (c + 1)])))
    return maps


def launchE2_inputs(x1full, e1res):
    maps = []
    for c in range(NCORES):
        sl = slice(1024 * c, 1024 * (c + 1))
        parts = np.stack([e1res[p]["part"][:, sl] for p in range(NCORES)])
        maps.append(dict(parts=parts,
                         x1own=np.ascontiguousarray(x1full[sl])))
    return maps


def launchG1_inputs(inputs, x3full):
    wgup = inputs["L1_ffn_w_gate_up"]
    wproj = inputs["L1_ffn_w_proj"]
    maps = []
    for c in range(NCORES):
        wgu = np.ascontiguousarray(np.concatenate(
            [wgup[1024 * c:1024 * (c + 1)],
             wgup[8192 + 1024 * c:8192 + 1024 * (c + 1)]], 0))
        wps = np.ascontiguousarray(wproj[:, 1024 * c:1024 * (c + 1)])
        maps.append(dict(x3f=x3full, wgu=wgu, wps=wps))
    return maps


def launchG2_inputs(x3full, g1res):
    maps = []
    for c in range(NCORES):
        sl = slice(1024 * c, 1024 * (c + 1))
        parts = np.stack([g1res[p]["partg"][:, sl] for p in range(NCORES)])
        ssqs = np.stack([g1res[p]["ssqo"][:, sl] for p in range(NCORES)])
        maps.append(dict(parts=parts, ssqs=ssqs,
                         x3own=np.ascontiguousarray(x3full[sl])))
    return maps


_BUILD_CACHE = {}


def _get(name, builder):
    if name not in _BUILD_CACHE:
        _BUILD_CACHE[name] = builder()
    return _BUILD_CACHE[name]


def run_launchA(inputs):
    nc = _get("A", build_launchA)
    maps = launchA_inputs(inputs)
    res = run_bass_kernel_spmd(nc, maps, core_ids=list(range(NCORES)))
    return res.results


def _run(name, builder, maps):
    nc = _get(name, builder)
    return run_bass_kernel_spmd(nc, maps, core_ids=list(range(NCORES))).results


def kernel(**inputs):
    ares = _run("A", build_launchA, launchA_inputs(inputs))
    cres = _run("C", build_launchC, launchC_inputs(inputs, ares))
    dres = _run("D", build_launchD, launchD_inputs(inputs, cres))
    x1full = np.concatenate([dres[c]["x1o"] for c in range(NCORES)], 0)
    e1res = _run("E1", build_launchE1, launchE1_inputs(inputs, x1full))
    e2res = _run("E2", build_launchE2, launchE2_inputs(x1full, e1res))
    x2full = np.concatenate([e2res[c]["x2o"] for c in range(NCORES)], 0)
    f1res = _run("F1", build_launchF1, launchF1_inputs(inputs, x2full))
    f2res = _run("F2", build_launchF2, launchF2_inputs(inputs, x2full, f1res))
    x3full = np.concatenate([f2res[c]["x3o"] for c in range(NCORES)], 0)
    g1res = _run("G1", build_launchG1, launchG1_inputs(inputs, x3full))
    g2res = _run("G2", build_launchG2, launchG2_inputs(x3full, g1res))
    x4 = np.concatenate([g2res[c]["x4o"] for c in range(NCORES)], 0)
    return x4.reshape(4, 2048, 2048)
